# version 05
# speedup vs baseline: 1.4531x; 1.0369x over previous
import math
from contextlib import ExitStack

import numpy as np
import concourse.bass as bass
import concourse.mybir as mybir
from concourse.bass_utils import run_bass_kernel_spmd

F32 = mybir.dt.float32
BF16 = mybir.dt.bfloat16
AF = mybir.ActivationFunctionType
ALU = mybir.AluOpType
AX = mybir.AxisListType

N_CORES = 8
B, T, D = 8, 4096, 1024



class Prog:

    COMPUTE = ("T", "V", "S", "G")
    DMAQ = ("DY", "DG", "DA")
    ISSUER = {"DY": "Y", "DG": "G", "DA": "S"}
    NLANES = 4

    def __init__(self, nc, ctx):
        self.nc = nc
        self.ops = []
        self.count = {e: 0 for e in self.COMPUTE}
        self.rr = {q: 0 for q in self.DMAQ}
        self.sem = {}
        for e in self.COMPUTE:
            self.sem[e] = ctx.enter_context(nc.semaphore("sem" + e))
        for q in self.DMAQ:
            for l in range(self.NLANES):
                s = q + str(l)
                self.sem[s] = ctx.enter_context(nc.semaphore("sem" + s))
                self.count[s] = 0
        self.last_write = {}
        self.readers = {}
        self.ns = None

    def op(self, eng, fn, reads=(), writes=(), tag=""):
        assert eng in self.COMPUTE
        self._record(eng, eng, fn, reads, writes, 1, tag)

    def dma(self, q, out_ap, in_ap, reads=(), writes=(), tag="", lane=None, **kw):
        assert q in self.DMAQ
        if lane is None:
            lane = self.rr[q]
            self.rr[q] = (lane + 1) % self.NLANES
        stream = q + str(lane % self.NLANES)
        def fn(e):
            return e.dma_start(out=out_ap, in_=in_ap, **kw)
        extra = [(stream, self.count[stream])] if self.count[stream] else []
        self._record(self.ISSUER[q], stream, fn, reads, writes, 16, tag,
                     extra_deps=extra)

    def _record(self, issue_eng, stream, fn, reads, writes, inc, tag,
                extra_deps=()):
        if self.ns is not None:
            reads = [(self.ns,) + tuple(r) for r in reads]
            writes = [(self.ns,) + tuple(w) for w in writes]
        deps = list(extra_deps)
        for r in reads:
            ev = self.last_write.get(r)
            if ev is not None:
                deps.append(ev)
        for w in writes:
            ev = self.last_write.get(w)
            if ev is not None:
                deps.append(ev)
            deps.extend(self.readers.get(w, ()))
        deps = [(s, self.count[s]) if s not in self.COMPUTE else (s, v)
                for (s, v) in deps]
        if stream == "T":
            deps = [d for d in deps if d[0] != "T"]
        self.count[stream] += inc
        ev = (stream, self.count[stream])
        for w in writes:
            self.last_write[w] = ev
            self.readers[w] = []
        for r in reads:
            self.readers.setdefault(r, []).append(ev)
        self.ops.append((issue_eng, stream, fn, _dedup(deps), inc, ev, tag))

    def barrier_event(self):
        return [(e, v) for e, v in self.count.items() if v > 0]

    def raw(self, eng, fn, tag=""):
        self.ops.append((eng, None, fn, [], 0, None, tag))

    def wait_all(self, eng):
        deps = self.barrier_event()
        self.ops.append((eng, None, None, deps, 0, None, "wait_all"))

    def barrier(self):
        for e in ("T", "V", "S", "G", "Y"):
            self.wait_all(e)

    def emit(self, block):
        streams = {"T": [], "V": [], "S": [], "G": [], "Y": []}
        seen = {e: {} for e in streams}
        for issue_eng, stream, fn, deps, inc, ev, tag in self.ops:
            streams[issue_eng].append((stream, fn, deps, inc, ev, tag))

        def body(eng_name):
            def run(eng):
                sn = seen[eng_name]
                for stream, fn, deps, inc, ev, tag in streams[eng_name]:
                    for (ds, dv) in deps:
                        if ds == stream == "T":
                            continue
                        if sn.get(ds, 0) < dv:
                            eng.wait_ge(self.sem[ds], dv)
                            sn[ds] = dv
                    if fn is None:
                        continue
                    ins = fn(eng)
                    if ev is not None:
                        ins.then_inc(self.sem[stream], inc)
            return run

        block.tensor(body("T"))
        block.vector(body("V"))
        block.scalar(body("S"))
        block.gpsimd(body("G"))
        block.sync(body("Y"))


def _dedup(deps):
    best = {}
    for s, v in deps:
        if best.get(s, 0) < v:
            best[s] = v
    return sorted(best.items())



class Env:
    def __init__(self, nc, ctx, p):
        self.nc = nc
        self.ctx = ctx
        self.p = p
        sb = self.sb
        self.ones_row_bf = sb("ones_row_bf", [1, 128], BF16)
        self.ones_col_bf = sb("ones_col_bf", [128, 1], BF16)
        self.ones_row_f = sb("ones_row_f", [1, 128], F32)
        self.ones_col_f = sb("ones_col_f", [128, 1], F32)
        self.ident_f = sb("ident_f", [128, 128], F32)
        self.ident_bf = sb("ident_bf", [128, 128], BF16)
        self.ps = [ctx.enter_context(nc.psum_tensor(f"psbank{i}", [128, 512], F32))
                   for i in range(8)]
        self.eps_row = sb("eps_row", [1, 1], F32)
        g = p.op
        g("G", lambda e: e.memset(self.eps_row[:, :], 1e-5), writes=[("epsc",)])
        g("G", lambda e: e.memset(self.ones_row_bf[:, :], 1.0), writes=[("ones_row_bf",)])
        g("G", lambda e: e.memset(self.ones_col_bf[:, :], 1.0), writes=[("ones_col_bf",)])
        g("G", lambda e: e.memset(self.ones_row_f[:, :], 1.0), writes=[("ones_row_f",)])
        g("G", lambda e: e.memset(self.ones_col_f[:, :], 1.0), writes=[("ones_col_f",)])
        for nm, t in (("ident_f", self.ident_f), ("ident_bf", self.ident_bf)):
            g("G", lambda e, t=t: e.memset(t[:, :], 0.0), writes=[(nm,)])
            g("G", lambda e, t=t: e.affine_select(
                out=t[:, :], in_=t[:, :], compare_op=ALU.not_equal, fill=1.0,
                base=0, pattern=[[-1, 128]], channel_multiplier=1),
              reads=[(nm,)], writes=[(nm,)])

    def sb(self, name, shape, dt):
        return self.ctx.enter_context(self.nc.sbuf_tensor(name, shape, dt))


POOL_COLS = 105600


def dram(nc, name, shape, dt, kind):
    memo = getattr(nc, "_dram_memo", None)
    if memo is None:
        memo = nc._dram_memo = {}
    if name not in memo:
        memo[name] = nc.dram_tensor(name, shape, dt, kind=kind).ap()
    return memo[name]


class _KeepOpen:
    def __init__(self, inner):
        self.inner = inner

    def __enter__(self):
        return self.inner

    def __exit__(self, *a):
        return False


class Pool:

    def __init__(self, env, cols):
        self.tB = env.ctx.enter_context(
            env.nc.sbuf_tensor("pool_arena", [128, cols], BF16))
        self.tF = self.tB.bitcast(F32)
        self.cols = cols
        self.off = 0

    def reset(self):
        self.off = 0

    def carve(self, cols, dt=BF16, parts=128):
        off = (self.off + 15) // 16 * 16
        if dt == F32:
            ap = self.tF[0:parts, off // 2: off // 2 + cols]
            self.off = off + 2 * cols
        else:
            ap = self.tB[0:parts, off:off + cols]
            self.off = off + cols
        assert self.off <= self.cols, f"pool overflow {self.off}>{self.cols}"
        return ap


def pool_ln_bufs(pool):
    return {
        "sq": [pool.carve(512) for _ in range(2)],
        "rows": [pool.carve(512, F32, parts=1) for _ in range(4)],
        "rowbf": [pool.carve(512, parts=1) for _ in range(2)],
        "ab": [pool.carve(512) for _ in range(2)],
        "cb": [pool.carve(512) for _ in range(2)],
        "tmp": [pool.carve(512) for _ in range(2)],
    }



def emit_ln(env, T, src, dst, rsrc, rdst, bufs, gamma=None, beta=None, eps=1e-5,
            nch=8, dmodel=1024, preload=None, post=None, rgam=(("lngam",),)):
    p = env.p
    ntt = T // 512
    sq, rows, ab, cb = bufs["sq"], bufs["rows"], bufs["ab"], bufs["cb"]
    rowbf = bufs["rowbf"]
    psS, psQ, psA, psC = env.ps[0], env.ps[1], env.ps[2], env.ps[3]
    for tt in range(ntt):
        if preload is not None:
            preload(tt)
        for dc in range(nch):
            sqt = sq[dc % 2]
            p.op("S", lambda e, dc=dc, tt=tt, sqt=sqt: e.activation(
                sqt[:, :], src(dc, tt), AF.Square),
                reads=[rsrc(dc, tt)], writes=[("lnsq", dc % 2)])
            p.op("T", lambda e, dc=dc, tt=tt: e.matmul(
                psS[0:1, :], env.ones_col_bf[:, :], src(dc, tt),
                start=(dc == 0), stop=(dc == nch - 1)),
                reads=[rsrc(dc, tt), ("ones_col_bf",)], writes=[("ps", 0)])
            p.op("T", lambda e, sqt=sqt, dc=dc: e.matmul(
                psQ[0:1, :], env.ones_col_bf[:, :], sqt[:, :],
                start=(dc == 0), stop=(dc == nch - 1)),
                reads=[("lnsq", dc % 2), ("ones_col_bf",)], writes=[("ps", 1)])
        mu, var, sd, r = rows[0], rows[1], rows[2], rows[3]
        c = var
        p.op("V", lambda e: e.tensor_scalar(mu[0:1, :], psS[0:1, :], 1.0 / dmodel,
                                            None, ALU.mult),
             reads=[("ps", 0)], writes=[("lnmu",)])
        p.op("V", lambda e: e.tensor_mul(var[0:1, :], mu[0:1, :], mu[0:1, :]),
             reads=[("lnmu",)], writes=[("lnvar",)])
        p.op("V", lambda e: e.scalar_tensor_tensor(var[0:1, :], psQ[0:1, :],
                                                   1.0 / dmodel, var[0:1, :],
                                                   ALU.mult, ALU.subtract),
             reads=[("ps", 1), ("lnvar",)], writes=[("lnvar",)])
        p.op("S", lambda e: e.activation(sd[0:1, :], var[0:1, :], AF.Sqrt,
                                         bias=env.eps_row[0:1, 0:1]),
             reads=[("lnvar",), ("epsc",)], writes=[("lnsd",)])
        p.op("V", lambda e: e.reciprocal(r[0:1, :], sd[0:1, :]),
             reads=[("lnsd",)], writes=[("lnr",)])
        p.op("V", lambda e: e.tensor_mul(c[0:1, :], mu[0:1, :], r[0:1, :]),
             reads=[("lnmu",), ("lnr",)], writes=[("lnc",)])
        p.op("V", lambda e: e.tensor_copy(rowbf[0][0:1, :], r[0:1, :]),
             reads=[("lnr",)], writes=[("lnrb",)])
        p.op("V", lambda e: e.tensor_copy(rowbf[1][0:1, :], c[0:1, :]),
             reads=[("lnc",)], writes=[("lncb",)])
        par = tt % 2
        p.op("T", lambda e: e.matmul(psA[:, :], env.ones_row_bf[:, :],
                                     rowbf[0][0:1, :], start=True, stop=True),
             reads=[("lnrb",), ("ones_row_bf",)], writes=[("ps", 2)])
        p.op("T", lambda e: e.matmul(psC[:, :], env.ones_row_bf[:, :],
                                     rowbf[1][0:1, :], start=True, stop=True),
             reads=[("lncb",), ("ones_row_bf",)], writes=[("ps", 3)])
        p.op("V", lambda e, par=par: e.tensor_copy(ab[par][:, :], psA[:, :]),
             reads=[("ps", 2)], writes=[("lnab", par)])
        p.op("V", lambda e, par=par: e.tensor_copy(cb[par][:, :], psC[:, :]),
             reads=[("ps", 3)], writes=[("lncbb", par)])
        tmp = bufs["tmp"]
        for dc in range(nch):
            dp = dc % 2
            p.op("V", lambda e, dc=dc, tt=tt, dp=dp, par=par: e.tensor_mul(
                tmp[dp][:, :], src(dc, tt), ab[par][:, :]),
                reads=[rsrc(dc, tt), ("lnab", par)], writes=[("lntmp", dp)])
            if gamma is None:
                p.op("G", lambda e, dc=dc, tt=tt, dp=dp, par=par: e.tensor_sub(
                    dst(dc, tt), tmp[dp][:, :], cb[par][:, :]),
                    reads=[("lntmp", dp), ("lncbb", par)], writes=[rdst(dc, tt)])
            else:
                p.op("G", lambda e, dc=dc, dp=dp, par=par: e.tensor_sub(
                    tmp[dp][:, :], tmp[dp][:, :], cb[par][:, :]),
                    reads=[("lntmp", dp), ("lncbb", par)], writes=[("lntmp", dp)])
                p.op("V", lambda e, dc=dc, tt=tt, dp=dp: e.tensor_scalar(
                    dst(dc, tt), tmp[dp][:, :], gamma[:, dc:dc + 1],
                    beta[:, dc:dc + 1], ALU.mult, ALU.add),
                    reads=[("lntmp", dp)] + list(rgam), writes=[rdst(dc, tt)])
            if post is not None:
                post(dc, tt)


def ln_bufs(env, prefix=""):
    return {
        "sq": [env.sb(prefix + "lnsq0", [128, 512], BF16),
               env.sb(prefix + "lnsq1", [128, 512], BF16)],
        "rows": [env.sb(prefix + f"lnrow{i}", [1, 512], F32) for i in range(4)],
        "rowbf": [env.sb(prefix + f"lnrowbf{i}", [1, 512], BF16) for i in range(2)],
        "ab": [env.sb(prefix + "lnabA", [128, 512], BF16),
               env.sb(prefix + "lnabB", [128, 512], BF16)],
        "cb": [env.sb(prefix + "lncbA", [128, 512], BF16),
               env.sb(prefix + "lncbB", [128, 512], BF16)],
        "tmp": [env.sb(prefix + "lntmpA", [128, 512], BF16),
                env.sb(prefix + "lntmpB", [128, 512], BF16)],
    }



def build_f0_program(T=4096, E=8, DFF=4096, fused=None, pfx=""):
    nch = 8
    ntt = T // 512
    nfc = DFF // 128
    nc = fused["nc"] if fused else bass.Bass("TRN2", target_bir_lowering=False,
                                             debug=False)
    x_h = fused["x"] if fused else dram(nc, pfx + "x", [D, T], F32, "ExternalInput")
    rw_h = dram(nc, pfx + "rw", [D, E], F32, "ExternalInput")
    rb_h = dram(nc, pfx + "rbias", [1, E], F32, "ExternalInput")
    up_h = dram(nc, pfx + "up", [E, D, DFF], F32, "ExternalInput")
    dn_h = dram(nc, pfx + "dn", [E, DFF, D], F32, "ExternalInput")
    y_h = fused["y"] if fused else dram(nc, pfx + "y", [D, T], F32, "ExternalOutput")
    xhat_h = dram(nc, pfx + "xhat_h", [D, T], BF16, "Internal")
    oacc_h = dram(nc, pfx + "oacc_h", [D, T], F32, "Internal")

    with (ExitStack() if fused is None else _KeepOpen(fused["ctx"])) as ctx:
        if fused is None:
            p = Prog(nc, ctx)
            env = Env(nc, ctx, p)
            pool = Pool(env, POOL_COLS)
            p.ns = "s"
        else:
            p, env, pool = fused["p"], fused["env"], fused["pool"]
            pool.reset()
            p.ns = pfx
        p.barrier()
        upb = pool.carve(nch * DFF, BF16, parts=128)
        dnb = pool.carve(nfc * D, BF16, parts=128)
        Hb = pool.carve(nfc * 512, BF16, parts=128)
        xg0 = pool.carve(nch * 512, BF16, parts=128)
        xg = [xg0, xg0]
        stage0 = pool.carve(1024, F32, parts=128)
        stage = [stage0, stage0]

        cbtbuf = pool.carve(1024, F32, parts=128)
        cbt = [cbtbuf[:, 0:512], cbtbuf[:, 512:1024]]
        comb = [pool.carve(E, F32, parts=128) for i in range(2)]
        combfm = pool.carve(T, BF16, parts=E)
        self_sel = pool.carve(E * 128, BF16, parts=E)
        rwb = pool.carve(nch * E, BF16, parts=128)
        rbb = pool.carve(E, F32, parts=128)
        logit = [pool.carve(E, F32, parts=128) for i in range(2)]
        lscr = [pool.carve(2 * E + 2, F32, parts=128) for i in range(2)]
        lnb = pool_ln_bufs(pool)

        xb = lambda dc, tt: dnb[:, dc * T + tt * 512: dc * T + (tt + 1) * 512]
        xh = lambda dc, tt: upb[:, dc * T + tt * 512: dc * T + (tt + 1) * 512]
        rxb = lambda dc, tt: ("dnw", dc)
        rxh = lambda dc, tt: ("up", dc)

        p.dma("DG", dnb[:, 0:nch * T].rearrange("p (c t) -> p c t", t=T),
              x_h.rearrange("(c p) t -> p c t", p=128),
              writes=[("dnw", dc) for dc in range(nch)])
        p.dma("DG", rwb[:, :].rearrange("p (c e) -> p c e", e=E),
              rw_h.rearrange("(c p) e -> p c e", p=128), writes=[("rw",)])
        emit_ln(env, T, xb, xh, rxb, rxh, lnb)
        p.dma("DY", xhat_h.rearrange("(c p) t -> p c t", p=128),
              upb[:, 0:nch * T].rearrange("p (c t) -> p c t", t=T),
              reads=[("up", dc) for dc in range(nch)], writes=[("xhH",)])

        rbbf = pool.carve(E, F32, parts=1)
        p.dma("DY", rbbf[:, :], rb_h, writes=[("rbf",)])
        p.op("T", lambda e: e.matmul(env.ps[4][:, 0:E], env.ones_row_f[:, :],
                                     rbbf[0:1, :], start=True, stop=True),
             reads=[("rbf",), ("ones_row_f",)], writes=[("ps", 4)])
        p.op("V", lambda e: e.tensor_copy(rbb[:, :], env.ps[4][:, 0:E]),
             reads=[("ps", 4)], writes=[("rbb",)])

        p.op("G", lambda e: e.memset(self_sel[:, :], 0.0), writes=[("sel",)])
        p.op("G", lambda e: e.affine_select(
            out=self_sel[:, :].rearrange("p (e m) -> p e m", m=128),
            in_=self_sel[:, :].rearrange("p (e m) -> p e m", m=128),
            compare_op=ALU.not_equal, fill=1.0, base=0,
            pattern=[[-1, E], [0, 128]], channel_multiplier=1),
            reads=[("sel",)], writes=[("sel",)])

        ntc = T // 128
        for tc in range(ntc):
            par = tc % 2
            tt = (tc * 128) // 512
            lg = logit[par]
            sc = lscr[par]
            p.op("T", lambda e, tc=tc, par=par: [
                e.matmul(env.ps[4 + par][:, 0:E],
                         upb[:, dc * T + tc * 128: dc * T + (tc + 1) * 128],
                         rwb[:, dc * E:(dc + 1) * E],
                         start=(dc == 0), stop=(dc == nch - 1))
                for dc in range(nch)][-1],
                reads=[("up", dc) for dc in range(nch)] + [("rw",)],
                writes=[("ps", 4 + par)])
            p.op("V", lambda e, lg=lg, par=par: e.tensor_add(
                lg[:, :], env.ps[4 + par][:, 0:E], rbb[:, :]),
                reads=[("ps", 4 + par), ("rbb",)], writes=[("lg", par)])
            p.op("V", lambda e, lg=lg, sc=sc: e.reduce_max(sc[:, 0:1], lg[:, :], AX.X),
                 reads=[("lg", par)], writes=[("lsc", par)])
            p.op("V", lambda e, lg=lg, sc=sc: e.tensor_scalar(
                sc[:, 2:2 + E], lg[:, :], sc[:, 0:1], None, ALU.subtract),
                reads=[("lg", par), ("lsc", par)], writes=[("lex", par)])
            p.op("S", lambda e, sc=sc: e.activation(sc[:, 2:2 + E], sc[:, 2:2 + E],
                                                    AF.Exp),
                 reads=[("lex", par)], writes=[("lex", par)])
            p.op("V", lambda e, sc=sc: e.reduce_sum(sc[:, 1:2], sc[:, 2:2 + E], AX.X),
                 reads=[("lex", par)], writes=[("lsum", par)])
            p.op("V", lambda e, sc=sc: e.reciprocal(sc[:, 1:2], sc[:, 1:2]),
                 reads=[("lsum", par)], writes=[("lsum", par)])
            p.op("V", lambda e, sc=sc: e.tensor_scalar(
                sc[:, 2:2 + E], sc[:, 2:2 + E], sc[:, 1:2], None, ALU.mult),
                reads=[("lex", par), ("lsum", par)], writes=[("lex", par)])
            p.op("V", lambda e, sc=sc: e.reduce_max(sc[:, 0:1], sc[:, 2:2 + E], AX.X),
                 reads=[("lex", par)], writes=[("lsc", par)])
            p.op("V", lambda e, sc=sc: e.tensor_scalar(
                sc[:, 2 + E:2 + 2 * E], sc[:, 2:2 + E], sc[:, 0:1], None, ALU.is_lt),
                reads=[("lex", par), ("lsc", par)], writes=[("lind", par)])
            p.op("V", lambda e, sc=sc: e.tensor_mul(
                sc[:, 2 + E:2 + 2 * E], sc[:, 2 + E:2 + 2 * E], sc[:, 2:2 + E]),
                reads=[("lind", par), ("lex", par)], writes=[("lind", par)])
            p.op("V", lambda e, sc=sc: e.reduce_max(
                sc[:, 1:2], sc[:, 2 + E:2 + 2 * E], AX.X),
                reads=[("lind", par)], writes=[("lsum", par)])
            p.op("V", lambda e, sc=sc: e.tensor_scalar(
                sc[:, 2 + E:2 + 2 * E], sc[:, 2:2 + E], sc[:, 1:2], None, ALU.is_ge),
                reads=[("lex", par), ("lsum", par)], writes=[("lind", par)])
            p.op("V", lambda e, sc=sc, par=par: e.tensor_mul(
                comb[par][:, :], sc[:, 2 + E:2 + 2 * E], sc[:, 2:2 + E]),
                reads=[("lind", par), ("lex", par)], writes=[("comb", par)])
            p.op("T", lambda e, tc=tc, par=par: e.transpose(
                env.ps[6 + par][0:E, 0:128], comb[par][:, :],
                env.ident_f[:, :]),
                reads=[("comb", par), ("ident_f",)], writes=[("ps", 6 + par)])
            p.op("V", lambda e, tc=tc, par=par: e.tensor_copy(
                combfm[0:E, tc * 128:(tc + 1) * 128], env.ps[6 + par][0:E, 0:128]),
                reads=[("ps", 6 + par)], writes=[("combfm", tc)])

        for e_i in range(E):
            for piece in range(8):
                p.dma("DG", upb[:, piece * DFF:(piece + 1) * DFF]
                      .rearrange("p (c f) -> p c f", c=1),
                      up_h[e_i].rearrange("(c p) f -> p c f", p=128)
                      [:, piece:piece + 1, :],
                      writes=[("up", piece)], tag=f"wup{e_i}")
                p.dma("DG", dnb[:, piece * (nfc // 8) * D:(piece + 1) * (nfc // 8) * D]
                      .rearrange("p (c d) -> p c d", d=D),
                      dn_h[e_i].rearrange("(c p) d -> p c d", p=128)
                      [:, piece * (nfc // 8):(piece + 1) * (nfc // 8), :],
                      writes=[("dnw", piece)], tag=f"wdn{e_i}")
            for tt in range(ntt):
                par = tt % 2
                sl = slice(tt * 512, (tt + 1) * 512)
                p.dma("DY", xg[par][:, :].rearrange("p (c t) -> p c t", t=512),
                      xhat_h.rearrange("(c p) t -> p c t", p=128)[:, :, sl],
                      reads=[("xhH",)], writes=[("xg", 0)])
                p.op("T", lambda e, e_i=e_i, sl=sl, par=par: e.matmul(
                    env.ps[0][:, :], self_sel[:, e_i * 128:(e_i + 1) * 128],
                    combfm[0:E, sl], start=True, stop=True),
                    reads=[("combfm", tc) for tc in range(tt * 4, tt * 4 + 4)]
                    + [("sel",)], writes=[("ps", 0)])
                p.op("V", lambda e, par=par: e.tensor_copy(cbt[par],
                                                           env.ps[0][:, :]),
                     reads=[("ps", 0)], writes=[("cbt", par)])
                for fc in range(nfc):
                    bk = fc % 2
                    p.op("T", lambda e, fc=fc, bk=bk, par=par: [
                        e.matmul(env.ps[bk][:, :],
                                 upb[:, dc * DFF + fc * 128: dc * DFF + (fc + 1) * 128],
                                 xg[par][:, dc * 512:(dc + 1) * 512],
                                 start=(dc == 0), stop=(dc == nch - 1))
                        for dc in range(nch)][-1],
                        reads=[("up", pi) for pi in range(8)] + [("xg", 0)],
                        writes=[("ps", bk)])
                    p.op("S", lambda e, fc=fc, bk=bk: e.activation(
                        Hb[:, fc * 512:(fc + 1) * 512], env.ps[bk][:, :], AF.Gelu),
                        reads=[("ps", bk)], writes=[("H", fc)])
                for dc in range(nch):
                    p.op("T", lambda e, dc=dc: [
                        e.matmul(env.ps[dc][:, :],
                                 dnb[:, fc * D + dc * 128: fc * D + (dc + 1) * 128],
                                 Hb[:, fc * 512:(fc + 1) * 512],
                                 start=(fc == 0), stop=(fc == nfc - 1))
                        for fc in range(nfc)][-1],
                        reads=[("dnw", pi) for pi in range(8)]
                        + [("H", fc) for fc in range(nfc)],
                        writes=[("ps", dc)])
                for dcp in range(nch // 2):
                    sp = dcp % 2
                    p.op("V", lambda e, dcp=dcp, sp=sp, par=par: [
                        e.tensor_mul(stage[sp][:, i * 512:(i + 1) * 512],
                                     env.ps[dcp * 2 + i][:, :], cbt[par])
                        for i in range(2)][-1],
                        reads=[("ps", dcp * 2), ("ps", dcp * 2 + 1), ("cbt", par)],
                        writes=[("stage", 0)])
                    p.dma("DG",
                          oacc_h.rearrange("(c p) t -> p c t", p=128)
                          [:, dcp * 2:dcp * 2 + 2, sl],
                          stage[sp][:, :].rearrange("p (c t) -> p c t", t=512),
                          reads=[("stage", 0)], writes=[("oacc", dcp, tt)],
                          **({} if e_i == 0 else {"accum_op": ALU.add}))

        fo = cbtbuf
        npc = T // 128
        for pp in range(npc):
            sl = slice(pp * 128, (pp + 1) * 128)
            par = pp % 2
            p.dma("DY", stage[par][:, :].rearrange("p (c t) -> p c t", t=128),
                  x_h.rearrange("(c p) t -> p c t", p=128)[:, :, sl],
                  reads=[], writes=[("stage", 0)])
            p.dma("DA", fo[:, :].rearrange("p (c t) -> p c t", t=128),
                  oacc_h.rearrange("(c p) t -> p c t", p=128)[:, :, sl],
                  reads=[("oacc", dcp, pp // 4) for dcp in range(4)],
                  writes=[("cbt", 0), ("cbt", 1)])
            p.op("V", lambda e, par=par: e.tensor_add(
                fo[:, :], stage[0][:, :], fo[:, :]),
                reads=[("stage", 0), ("cbt", 0), ("cbt", 1)],
                writes=[("cbt", 0), ("cbt", 1)])
            p.dma("DY", y_h.rearrange("(c p) t -> p c t", p=128)[:, :, sl],
                  fo[:, :].rearrange("p (c t) -> p c t", t=128),
                  reads=[("cbt", 0), ("cbt", 1)], writes=[("yH", pp)])
        if fused is None:
            p.wait_all("Y")
            block = ctx.enter_context(nc.Block())
            p.emit(block)
    return nc



def build_b0_program(T=4096, max_pairs=8, max_hh=2, max_qt=99, dbg=None, fused=None, pfx=""):
    nch = 8
    ntt = T // 512
    ntc = T // 128
    nc = fused["nc"] if fused else bass.Bass("TRN2", target_bir_lowering=False,
                                             debug=False)
    x_h = fused["x"] if fused else dram(nc, pfx + "x", [D, T], F32, "ExternalInput")
    wdn_h = dram(nc, pfx + "wdn", [D, D], F32, "ExternalInput")
    tokp_h = dram(nc, pfx + "tokp", [128, 512], F32, "ExternalInput")
    toqp_h = dram(nc, pfx + "toqp", [128, 512], F32, "ExternalInput")
    tovp_h = dram(nc, pfx + "tovp", [128, 512], F32, "ExternalInput")
    wu_h = dram(nc, pfx + "wu", [D, D], F32, "ExternalInput")
    bu_h = dram(nc, pfx + "bu2", [128, 8], F32, "ExternalInput")
    y_h = fused["y"] if fused else dram(nc, pfx + "y", [D, T], F32, "ExternalOutput")

    SCL = float(64.0 ** -0.5)

    with (ExitStack() if fused is None else _KeepOpen(fused["ctx"])) as ctx:
        if fused is None:
            p = Prog(nc, ctx)
            env = Env(nc, ctx, p)
            pool = Pool(env, POOL_COLS)
            p.ns = "s"
        else:
            p, env, pool = fused["p"], fused["env"], fused["pool"]
            pool.reset()
            p.ns = pfx
        p.barrier()
        xdb = pool.carve(nch * T, BF16, parts=128)
        xhb = pool.carve(nch * T, BF16, parts=128)
        wdb = pool.carve(nch * D, BF16, parts=128)
        tokb = pool.carve(512, BF16, parts=128)
        toqb = pool.carve(512, BF16, parts=128)
        tovb = pool.carve(512, BF16, parts=128)
        qhb = pool.carve(T, BF16, parts=128)
        khb = pool.carve(T, BF16, parts=128)
        vhb = pool.carve(ntc * 130, BF16, parts=128)
        ptb = [pool.carve(512, BF16, parts=128) for i in range(4)]
        sqp = [pool.carve(512, BF16, parts=128) for i in range(2)]
        nrm = pool.carve(512 + 16, F32, parts=1)
        negc = pool.carve(2, F32, parts=128)
        rrow = pool.carve(512, BF16, parts=1)
        rbt = [pool.carve(512, BF16, parts=128) for i in range(2)]
        bu_sb = pool.carve(8, F32, parts=128)
        fx = pool.carve(1024, F32, parts=128)
        lnb = pool_ln_bufs(pool)

        xb = lambda dc, tt: xdb[:, dc * T + tt * 512: dc * T + (tt + 1) * 512]
        xh = lambda dc, tt: xhb[:, dc * T + tt * 512: dc * T + (tt + 1) * 512]
        rxb = lambda dc, tt: ("xdb", dc)
        rxh = lambda dc, tt: ("xhb", dc)

        p.dma("DG", xdb[:, 0:nch * T].rearrange("p (c t) -> p c t", t=T),
              x_h.rearrange("(c p) t -> p c t", p=128),
              writes=[("xdb", dc) for dc in range(nch)])
        p.dma("DG", wdb[:, :].rearrange("p (c n) -> p c n", n=D),
              wdn_h.rearrange("(c p) n -> p c n", p=128),
              writes=[("wdb", dc) for dc in range(nch)])
        p.dma("DG", tokb[:, :], tokp_h, writes=[("tok",)])
        p.dma("DG", toqb[:, :], toqp_h, writes=[("toq",)])
        p.dma("DG", tovb[:, :], tovp_h, writes=[("tov",)])
        p.dma("DY", bu_sb[:, :], bu_h, writes=[("bu",)])

        emit_ln(env, T, xb, xh, rxb, rxh, lnb)

        for dc_o in range(nch):
            for tt in range(ntt):
                bk = 4 + (dc_o * ntt + tt) % 2
                sl = slice(tt * 512, (tt + 1) * 512)
                p.op("T", lambda e, dc_o=dc_o, sl=sl, bk=bk: [
                    e.matmul(env.ps[bk][:, :],
                             wdb[:, dc * D + dc_o * 128: dc * D + (dc_o + 1) * 128],
                             xhb[:, dc * T:(dc + 1) * T][:, sl],
                             start=(dc == 0), stop=(dc == nch - 1))
                    for dc in range(nch)][-1],
                    reads=[("wdb", dc) for dc in range(nch)]
                    + [("xhb", dc) for dc in range(nch)],
                    writes=[("ps", bk)])
                p.op("V", lambda e, dc_o=dc_o, sl=sl, bk=bk: e.tensor_copy(
                    xdb[:, dc_o * T:(dc_o + 1) * T][:, sl], env.ps[bk][:, :]),
                    reads=[("ps", bk)], writes=[("xdb", dc_o)])

        p.dma("DG", wdb[:, :].rearrange("p (c n) -> p c n", n=D),
              wu_h.rearrange("(c p) n -> p c n", p=128),
              writes=[("wdb", dc) for dc in range(nch)])

        sclc = pool.carve(1, F32, parts=128)
        rscr = nrm[0:1, 0:512]
        p.op("G", lambda e: e.memset(sclc[:, :], SCL), writes=[("sclc",)])
        p.op("G", lambda e: e.memset(vhb[:, :], 1.0),
             writes=[("vh", tc) for tc in range(ntc)])
        for hp in range(min(nch, max_pairs)):
            xd_hp = xdb[:, hp * T:(hp + 1) * T]
            rk = ("xdb", hp)
            for dst, mix, rmix, rdst in ((khb, tokb, ("tok",), ("kh",)),
                                         (qhb, toqb, ("toq",), ("qh",))):
                for tt in range(ntt):
                    bk = 4 + tt % 2
                    sl = slice(tt * 512, (tt + 1) * 512)
                    p.op("T", lambda e, mix=mix, sl=sl, bk=bk, hp=hp, xd=xd_hp: [
                        e.matmul(env.ps[bk][0:64, :],
                                 mix[0:64, hp * 64:(hp + 1) * 64],
                                 xd[0:64, sl], start=True, stop=True,
                                 tile_position=(0, 0)),
                        e.matmul(env.ps[bk][64:128, :],
                                 mix[64:128, hp * 64:(hp + 1) * 64],
                                 xd[64:128, sl], start=True, stop=True,
                                 tile_position=(64, 64))][-1],
                        reads=[rmix, rk], writes=[("ps", bk)])
                    p.op("V", lambda e, dst=dst, sl=sl, bk=bk: e.tensor_copy(
                        dst[:, sl], env.ps[bk][:, :]),
                        reads=[("ps", bk)], writes=[rdst])
            for tc in range(ntc):
                bk = 4 + tc % 2
                bk2 = 6 + tc % 2
                p.op("T", lambda e, tc=tc, bk=bk, bk2=bk2, hp=hp, xd=xd_hp: [
                    e.matmul(env.ps[bk][0:128, 0:64],
                             xd[0:64, tc * 128:(tc + 1) * 128],
                             tovb[0:64, hp * 64:(hp + 1) * 64],
                             start=True, stop=True, tile_position=(0, 0)),
                    e.matmul(env.ps[bk2][0:128, 0:64],
                             xd[64:128, tc * 128:(tc + 1) * 128],
                             tovb[64:128, hp * 64:(hp + 1) * 64],
                             start=True, stop=True, tile_position=(64, 0))][-1],
                    reads=[("tov",), rk], writes=[("ps", bk), ("ps", bk2)])
                p.op("V", lambda e, tc=tc, bk=bk, bk2=bk2: [
                    e.tensor_copy(vhb[:, tc * 130:tc * 130 + 64],
                                  env.ps[bk][:, 0:64]),
                    e.tensor_copy(vhb[:, tc * 130 + 65:tc * 130 + 129],
                                  env.ps[bk2][:, 0:64])][-1],
                    reads=[("ps", bk), ("ps", bk2)], writes=[("vh", tc)])
            for hh in range(max_hh):
                base = hh * 64
                tp = (base, 0)
                NS = 512
                for src, so, rs in ((khb, 0, ("kh",)), (qhb, 1, ("qh",))):
                    for tt in range(ntt):
                        sq = sqp[tt % 2]
                        sl = slice(tt * 512, (tt + 1) * 512)
                        p.op("S", lambda e, src=src, sl=sl, base=base, sq=sq:
                             e.activation(sq[0:64, :], src[base:base + 64, sl],
                                          AF.Square),
                             reads=[rs], writes=[("atsq", tt % 2)])
                        p.op("T", lambda e, sq=sq: e.matmul(
                            env.ps[6][0:1, 0:512], env.ones_col_bf[0:64, :],
                            sq[0:64, :], start=True, stop=True,
                            tile_position=(0, 0)),
                            reads=[("atsq", tt % 2), ("ones_col_bf",)],
                            writes=[("ps", 6)])
                        if tt == 0:
                            p.op("V", lambda e, so=so: e.reduce_max(
                                nrm[0:1, NS + so:NS + so + 1],
                                env.ps[6][0:1, 0:512], AX.X),
                                reads=[("ps", 6)], writes=[("nmx", so)])
                        else:
                            p.op("V", lambda e, so=so: e.reduce_max(
                                nrm[0:1, NS + 4:NS + 5],
                                env.ps[6][0:1, 0:512], AX.X),
                                reads=[("ps", 6)], writes=[("nmt",)])
                            p.op("V", lambda e, so=so: e.tensor_max(
                                nrm[0:1, NS + so:NS + so + 1],
                                nrm[0:1, NS + so:NS + so + 1],
                                nrm[0:1, NS + 4:NS + 5]),
                                reads=[("nmt",), ("nmx", so)],
                                writes=[("nmx", so)])
                for so in (0, 1):
                    p.op("S", lambda e, so=so: e.activation(
                        nrm[0:1, NS + 2 + so:NS + 3 + so],
                        nrm[0:1, NS + so:NS + so + 1], AF.Sqrt),
                        reads=[("nmx", so)], writes=[("nsq", so)])
                p.op("V", lambda e: e.tensor_mul(
                    nrm[0:1, NS + 5:NS + 6], nrm[0:1, NS + 2:NS + 3],
                    nrm[0:1, NS + 3:NS + 4]),
                    reads=[("nsq", 0), ("nsq", 1)], writes=[("ncc",)])
                p.op("V", lambda e: e.tensor_scalar(
                    nrm[0:1, NS + 6:NS + 7], nrm[0:1, NS + 5:NS + 6],
                    -SCL, None, ALU.mult),
                    reads=[("ncc",)], writes=[("nbc",)])
                p.op("T", lambda e: e.matmul(env.ps[6][:, 0:1], env.ones_row_f[:, :],
                                             nrm[0:1, NS + 6:NS + 7],
                                             start=True, stop=True),
                     reads=[("nbc",), ("ones_row_f",)], writes=[("ps", 6)])
                p.op("V", lambda e, hh=hh: e.tensor_copy(negc[:, hh:hh + 1],
                                                         env.ps[6][:, 0:1]),
                     reads=[("ps", 6)], writes=[("negc", hh)])
                def norm_b0(qt, abk, qsl, base=base, hp=hp):
                    p.op("V", lambda e, abk=abk: e.reciprocal(
                        rscr[0:1, :], env.ps[abk][64:65, :]),
                        reads=[("ps", abk)], writes=[("nrec",)])
                    p.op("V", lambda e: e.tensor_copy(rrow[0:1, :],
                                                      rscr[0:1, :]),
                         reads=[("nrec",)], writes=[("nrecb",)])
                    p.op("T", lambda e, qt=qt: e.matmul(
                        env.ps[qt % 2][0:64, :], env.ones_row_bf[:, 0:64],
                        rrow[0:1, :], start=True, stop=True),
                        reads=[("nrecb",), ("ones_row_bf",)],
                        writes=[("ps", qt % 2)])
                    p.op("V", lambda e, qt=qt: e.tensor_copy(
                        rbt[qt % 2][0:64, :], env.ps[qt % 2][0:64, :]),
                        reads=[("ps", qt % 2)], writes=[("rbt", qt % 2)])
                    p.op("V", lambda e, qt=qt, qsl=qsl, abk=abk, base=base,
                         hp=hp: e.tensor_mul(
                             xhb[:, hp * T:(hp + 1) * T][base:base + 64, qsl],
                             env.ps[abk][0:64, :], rbt[qt % 2][0:64, :]),
                         reads=[("ps", abk), ("rbt", qt % 2)],
                         writes=[("xhb", hp)])
                pend = None
                for qt in range(min(ntt, max_qt)):
                    qsl = slice(qt * 512, (qt + 1) * 512)
                    abk = 6 + qt % 2
                    voff = 0 if base == 0 else 65
                    LA = 2
                    for step in range(ntc + LA):
                      if step == 4 and pend is not None:
                        norm_b0(*pend)
                        pend = None
                      if step < ntc:
                        kc = step
                        sbk = 2 + kc % 4
                        pt = ptb[kc % 4]
                        p.op("T", lambda e, kc=kc, qsl=qsl, base=base, sbk=sbk, tp=tp:
                             e.matmul(env.ps[sbk][:, :],
                                      khb[base:base + 64,
                                          kc * 128:(kc + 1) * 128],
                                      qhb[base:base + 64, qsl],
                                      start=True, stop=True, tile_position=tp),
                             reads=[("kh",), ("qh",)], writes=[("ps", sbk)])
                        p.op("S", lambda e, pt=pt, sbk=sbk, hh=hh: e.activation(
                            pt[:, :], env.ps[sbk][:, :], AF.Exp,
                            bias=negc[:, hh:hh + 1], scale=sclc[:, 0:1]),
                            reads=[("ps", sbk), ("negc", hh), ("sclc",)],
                            writes=[("pt", kc % 4)])
                      if step >= LA:
                        kc = step - LA
                        pt = ptb[kc % 4]
                        p.op("T", lambda e, pt=pt, kc=kc, voff=voff, abk=abk:
                             e.matmul(env.ps[abk][0:65, :],
                                      vhb[:, kc * 130 + voff:kc * 130 + voff + 65],
                                      pt[:, :],
                                      start=(kc == 0), stop=(kc == ntc - 1)),
                             reads=[("pt", kc % 4), ("vh", kc)],
                             writes=[("ps", abk)])
                    pend = (qt, abk, qsl)
                norm_b0(*pend)

        if dbg == "oT":
            dbg_h = dram(nc, pfx + "dbg", [D, T], F32, "ExternalOutput")
            dbf = pool.carve(512, F32, parts=128)
            for dc in range(nch):
                for tt in range(ntt):
                    sl = slice(tt * 512, (tt + 1) * 512)
                    p.op("V", lambda e, dc=dc, sl=sl: e.tensor_copy(
                        dbf[:, :], xhb[:, dc * T:(dc + 1) * T][:, sl]),
                        reads=[("xhb", dc)], writes=[("dbf",)])
                    p.dma("DY", dbg_h.rearrange("(c p) t -> p c t", p=128)
                          [:, dc:dc + 1, sl],
                          dbf[:, :].rearrange("p (c t) -> p c t", c=1),
                          reads=[("dbf",)], writes=[("dbgH", dc, tt)])
        if dbg == "kqv":
            dbg_h = dram(nc, pfx + "dbg", [3 * 128, T], F32, "ExternalOutput")
            dbf = pool.carve(512, F32, parts=128)
            for i, (srcb, rr) in enumerate(((khb, ("kh",)), (qhb, ("qh",)))):
                for tt in range(ntt):
                    sl = slice(tt * 512, (tt + 1) * 512)
                    p.op("V", lambda e, srcb=srcb, sl=sl: e.tensor_copy(
                        dbf[:, :], srcb[:, sl]), reads=[rr], writes=[("dbf",)])
                    p.dma("DY", dbg_h.rearrange("(c p) t -> p c t", p=128)
                          [:, i:i + 1, sl],
                          dbf[:, :].rearrange("p (c t) -> p c t", c=1),
                          reads=[("dbf",)], writes=[("dbgH", i, tt)])
            for tc in range(T // 128):
                p.op("V", lambda e, tc=tc: e.tensor_copy(
                    dbf[:, 0:130], vhb[:, tc * 130:(tc + 1) * 130]),
                    reads=[("vh", tc)], writes=[("dbf",)])
                p.dma("DY", dbg_h[256:384, tc * 128:tc * 128 + 130]
                      if False else
                      dbg_h.rearrange("(c p) t -> p c t", p=128)
                      [:, 2:3, tc * 128:tc * 128 + 130 - 2],
                      dbf[:, 0:128].rearrange("p (c t) -> p c t", c=1),
                      reads=[("dbf",)], writes=[("dbgH", 2, tc)])
        for dc_o in range(nch):
            for tt in range(ntt):
                bk = 4 + (dc_o * ntt + tt) % 2
                sl = slice(tt * 512, (tt + 1) * 512)
                p.op("T", lambda e, dc_o=dc_o, sl=sl, bk=bk: [
                    e.matmul(env.ps[bk][:, :],
                             wdb[:, dc * D + dc_o * 128: dc * D + (dc_o + 1) * 128],
                             xhb[:, dc * T:(dc + 1) * T][:, sl],
                             start=(dc == 0), stop=(dc == nch - 1))
                    for dc in range(nch)][-1],
                    reads=[("wdb", dc) for dc in range(nch)]
                    + [("xhb", dc) for dc in range(nch)],
                    writes=[("ps", bk)])
                p.op("S", lambda e, dc_o=dc_o, sl=sl, bk=bk: e.activation(
                    fx[:, 0:512], env.ps[bk][:, :], AF.Identity,
                    bias=bu_sb[:, dc_o:dc_o + 1]),
                    reads=[("ps", bk), ("bu",)], writes=[("fx0",)])
                p.dma("DY", fx[:, 512:1024].rearrange("p (c t) -> p c t", c=1),
                      x_h.rearrange("(c p) t -> p c t", p=128)
                      [:, dc_o:dc_o + 1, sl],
                      writes=[("fx1",)])
                p.op("V", lambda e: e.tensor_add(fx[:, 0:512], fx[:, 0:512],
                                                 fx[:, 512:1024]),
                     reads=[("fx0",), ("fx1",)], writes=[("fx0",)])
                p.dma("DY", y_h.rearrange("(c p) t -> p c t", p=128)
                      [:, dc_o:dc_o + 1, sl],
                      fx[:, 0:512].rearrange("p (c t) -> p c t", c=1),
                      reads=[("fx0",)], writes=[("yH", dc_o, tt)])
        if fused is None:
            p.wait_all("Y")
            block = ctx.enter_context(nc.Block())
            p.emit(block)
    return nc



def build_a0_program(T=4096, fused=None, pfx=""):
    nch = 8
    ntt = T // 512
    ntc = T // 128
    SCL = 0.125
    nc = fused["nc"] if fused else bass.Bass("TRN2", target_bir_lowering=False,
                                             debug=False)
    x_h = fused["x"] if fused else dram(nc, pfx + "x", [D, T], F32, "ExternalInput")
    wqkv_h = dram(nc, pfx + "wqkv", [D, 3 * D], F32, "ExternalInput")
    bqkv2_h = dram(nc, pfx + "bqkv2", [128, 24], F32, "ExternalInput")
    bqkvr_h = dram(nc, pfx + "bqkvr", [1, 3 * D], F32, "ExternalInput")
    wo_h = dram(nc, pfx + "wo", [D, D], F32, "ExternalInput")
    bo2_h = dram(nc, pfx + "bo2", [128, 8], F32, "ExternalInput")
    w1_h = dram(nc, pfx + "w1", [D, 4 * D], F32, "ExternalInput")
    b1_h = dram(nc, pfx + "b1_2", [128, 32], F32, "ExternalInput")
    w2_h = dram(nc, pfx + "w2", [4 * D, D], F32, "ExternalInput")
    b2_h = dram(nc, pfx + "b2_2", [128, 8], F32, "ExternalInput")
    g1_h = dram(nc, pfx + "ln1g2", [128, 8], F32, "ExternalInput")
    be1_h = dram(nc, pfx + "ln1b2", [128, 8], F32, "ExternalInput")
    g2_h = dram(nc, pfx + "ln2g2", [128, 8], F32, "ExternalInput")
    be2_h = dram(nc, pfx + "ln2b2", [128, 8], F32, "ExternalInput")
    cos_h = dram(nc, pfx + "cosb", [128, T], F32, "ExternalInput")
    sin_h = dram(nc, pfx + "sinb", [128, T], F32, "ExternalInput")
    rotm_h = dram(nc, pfx + "rotm", [128, 128], F32, "ExternalInput")
    y_h = fused["y"] if fused else dram(nc, pfx + "y", [D, T], F32, "ExternalOutput")
    xh1_h = dram(nc, pfx + "xh1_h", [D, T], BF16, "Internal")
    xa1_h = dram(nc, pfx + "xa1_h", [D, T], F32, "Internal")
    xh2_h = dram(nc, pfx + "xh2_h", [D, T], BF16, "Internal")

    with (ExitStack() if fused is None else _KeepOpen(fused["ctx"])) as ctx:
        if fused is None:
            p = Prog(nc, ctx)
            env = Env(nc, ctx, p)
            pool = Pool(env, POOL_COLS)
            p.ns = "s"
        else:
            p, env, pool = fused["p"], fused["env"], fused["pool"]
            pool.reset()
            p.ns = pfx
        p.barrier()
        A = pool.carve(max(nch * T, 32 * D), BF16, parts=128)
        U = pool.carve(16640, BF16, parts=128)
        qhb = U[:, 0:4096]
        khb = U[:, 4096:8192]
        vhb = U[:, 8192:12352]
        oTp = U[:, 12352:16448]
        Hslice = lambda fc: U[:, fc * 512:(fc + 1) * 512]
        lnsrc = [pool.carve(nch * 512, BF16, parts=128) for i in range(2)]
        xstr = lnsrc
        cosb = pool.carve(T, BF16, parts=128)
        sinb = pool.carve(T, BF16, parts=128)
        rotmb = pool.carve(128, BF16, parts=128)
        wo_sb = [pool.carve(D, BF16, parts=128) for i in range(2)]
        wsl = [[pool.carve(nch * 128, BF16, parts=128) for j in range(3)]
               for i in range(2)]
        w1s = [pool.carve(nch * 128, BF16, parts=128) for i in range(2)]
        qraw = [pool.carve(512, BF16, parts=128) for i in range(2)]
        rtmp = [pool.carve(512, BF16, parts=128) for i in range(2)]
        rtm2 = [pool.carve(512, BF16, parts=128) for i in range(2)]
        ptb = [pool.carve(512, BF16, parts=128) for i in range(4)]
        sqp = [pool.carve(512, BF16, parts=128) for i in range(2)]
        nrm = pool.carve(512 + 16, F32, parts=1)
        negc = pool.carve(2, F32, parts=128)
        sclc = pool.carve(1, F32, parts=128)
        rscr = nrm[0:1, 0:512]
        rrow = pool.carve(512, BF16, parts=1)
        rbt = [pool.carve(512, BF16, parts=128) for i in range(2)]
        fx = [pool.carve(512, F32, parts=128) for i in range(2)]
        fr = [pool.carve(512, F32, parts=128) for i in range(2)]
        ostg = [pool.carve(512, BF16, parts=128) for i in range(2)]
        bqkv_sb = pool.carve(24, F32, parts=128)
        bqr_sb = pool.carve(3 * D, BF16, parts=1)
        bo_sb = pool.carve(8, F32, parts=128)
        b1_sb = pool.carve(32, F32, parts=128)
        b2_sb = pool.carve(8, F32, parts=128)
        g1_sb = pool.carve(8, F32, parts=128)
        be1_sb = pool.carve(8, F32, parts=128)
        g2_sb = pool.carve(8, F32, parts=128)
        be2_sb = pool.carve(8, F32, parts=128)
        lnb = pool_ln_bufs(pool)

        NS = 512

        p.op("G", lambda e: e.memset(sclc[:, :], SCL), writes=[("sclc",)])
        p.dma("DG", cosb[:, :], cos_h, writes=[("cos",)])
        p.dma("DG", sinb[:, :], sin_h, writes=[("sin",)])
        p.dma("DG", rotmb[:, :], rotm_h, writes=[("rotm",)])
        p.dma("DG", bqr_sb[:, :], bqkvr_h, writes=[("bqr",)])
        for t_, h_, r_ in ((bqkv_sb, bqkv2_h, "bqkv"), (bo_sb, bo2_h, "bo"),
                           (b1_sb, b1_h, "b1"), (b2_sb, b2_h, "b2"),
                           (g1_sb, g1_h, "g1"), (be1_sb, be1_h, "be1"),
                           (g2_sb, g2_h, "g2"), (be2_sb, be2_h, "be2")):
            p.dma("DY", t_[:, :], h_, writes=[(r_,)])

        def pre1(tt):
            p.dma("DG", lnsrc[tt % 2][:, :].rearrange("p (c t) -> p c t", t=512),
                  x_h.rearrange("(c p) t -> p c t", p=128)
                  [:, :, tt * 512:(tt + 1) * 512],
                  writes=[("st", tt % 2)])
        emit_ln(env, T,
                src=lambda dc, tt: lnsrc[tt % 2][:, dc * 512:(dc + 1) * 512],
                dst=lambda dc, tt: A[:, dc * T + tt * 512: dc * T + (tt + 1) * 512],
                rsrc=lambda dc, tt: ("st", tt % 2),
                rdst=lambda dc, tt: ("A", dc),
                bufs=lnb, preload=pre1)

        def post1(dc, tt):
            p.dma("DY", xh1_h.rearrange("(c p) t -> p c t", p=128)
                  [:, dc:dc + 1, tt * 512:(tt + 1) * 512],
                  ostg[dc % 2][:, :].rearrange("p (c t) -> p c t", c=1),
                  reads=[("ostg", dc % 2)], writes=[("xh1H", dc, tt)])
        emit_ln(env, T,
                src=lambda dc, tt: A[:, dc * T + tt * 512: dc * T + (tt + 1) * 512],
                dst=lambda dc, tt: ostg[dc % 2][:, :],
                rsrc=lambda dc, tt: ("A", dc),
                rdst=lambda dc, tt: ("ostg", dc % 2),
                bufs=lnb, gamma=g1_sb, beta=be1_sb, rgam=[("g1",), ("be1",)], post=post1)

        p.op("G", lambda e: e.memset(vhb[:, :], 1.0),
             writes=[("vh", tc) for tc in range(ntc)])
        for hp in range(nch):
            wp = wsl[hp % 2]
            p.dma("DG", wo_sb[hp % 2][:, :].rearrange("p (c n) -> p c n", c=1),
                  wo_h.rearrange("(c p) n -> p c n", p=128)[:, hp:hp + 1, :],
                  writes=[("wo", hp % 2)], tag=f"wo{hp}", lane=0)
            for j, coff in enumerate((hp * 128, D + hp * 128, 2 * D + hp * 128)):
                p.dma("DG", wp[j][:, :].rearrange("p (c n) -> p c n", n=128),
                      wqkv_h.rearrange("(c p) n -> p c n", p=128)
                      [:, :, coff:coff + 128],
                      writes=[("wsl", hp % 2, j)], tag=f"wqkv{hp}", lane=1)
            for tt in range(ntt):
                sl = slice(tt * 512, (tt + 1) * 512)
                p.dma("DY" if tt % 2 == 0 else "DA",
                      xstr[tt % 2][:, :].rearrange("p (c t) -> p c t", t=512),
                      xh1_h.rearrange("(c p) t -> p c t", p=128)[:, :, sl],
                      reads=[("xh1H", dc, tt) for dc in range(nch)],
                      writes=[("st", tt % 2)])
                for j, dstb, rd, bcol in ((0, qhb, ("qh",), hp),
                                          (1, khb, ("kh",), 8 + hp)):
                    bk = 4 + j
                    p.op("T", lambda e, j=j, bk=bk, tt=tt, wp=wp: [
                        e.matmul(env.ps[bk][:, :],
                                 wp[j][:, dc * 128:(dc + 1) * 128],
                                 xstr[tt % 2][:, dc * 512:(dc + 1) * 512],
                                 start=(dc == 0), stop=(dc == nch - 1))
                        for dc in range(nch)][-1],
                        reads=[("wsl", hp % 2, j), ("st", tt % 2)],
                        writes=[("ps", bk)])
                    p.op("S", lambda e, bk=bk, tt=tt, bcol=bcol: e.activation(
                        qraw[tt % 2][:, :], env.ps[bk][:, :], AF.Identity,
                        bias=bqkv_sb[:, bcol:bcol + 1]),
                        reads=[("ps", bk), ("bqkv",)], writes=[("qraw", tt % 2)])
                    p.op("T", lambda e, bk=bk, tt=tt: e.matmul(
                        env.ps[bk][:, :], rotmb[:, :], qraw[tt % 2][:, :],
                        start=True, stop=True),
                        reads=[("rotm",), ("qraw", tt % 2)], writes=[("ps", bk)])
                    p.op("G", lambda e, tt=tt, sl=sl: e.tensor_mul(
                        rtmp[tt % 2][:, :], qraw[tt % 2][:, :], cosb[:, sl]),
                        reads=[("qraw", tt % 2), ("cos",)],
                        writes=[("rtmp", tt % 2)])
                    p.op("V", lambda e, bk=bk, tt=tt, sl=sl: e.tensor_mul(
                        rtm2[tt % 2][:, :], env.ps[bk][:, :], sinb[:, sl]),
                        reads=[("ps", bk), ("sin",)], writes=[("rtm2", tt % 2)])
                    p.op("V", lambda e, dstb=dstb, tt=tt, sl=sl: e.tensor_add(
                        dstb[:, sl], rtmp[tt % 2][:, :], rtm2[tt % 2][:, :]),
                        reads=[("rtmp", tt % 2), ("rtm2", tt % 2)], writes=[rd])
                for tq in range(4):
                    tc = tt * 4 + tq
                    bk = 6 + tc % 2
                    p.op("T", lambda e, tc=tc, tq=tq, bk=bk, tt=tt, wp=wp, hp=hp: [
                        e.matmul(env.ps[bk][:, 0:128],
                                 xstr[tt % 2][:, dc * 512 + tq * 128:
                                              dc * 512 + (tq + 1) * 128],
                                 wp[2][:, dc * 128:(dc + 1) * 128],
                                 start=(dc == 0), stop=False)
                        for dc in range(nch)] and
                        e.matmul(env.ps[bk][:, 0:128], env.ones_row_bf[:, :],
                                 bqr_sb[0:1, 2 * D + hp * 128:2 * D + (hp + 1) * 128],
                                 start=False, stop=True),
                        reads=[("wsl", hp % 2, 2), ("st", tt % 2), ("bqr",),
                               ("ones_row_bf",)],
                        writes=[("ps", bk)])
                    p.op("V", lambda e, tc=tc, bk=bk: [
                        e.tensor_copy(vhb[:, tc * 130:tc * 130 + 64],
                                      env.ps[bk][:, 0:64]),
                        e.tensor_copy(vhb[:, tc * 130 + 65:tc * 130 + 129],
                                      env.ps[bk][:, 64:128])][-1],
                        reads=[("ps", bk)], writes=[("vh", tc)])
            for hh in range(2):
                base = hh * 64
                tp = (base, 0)
                for src_b, so, rs in ((khb, 0, ("kh",)), (qhb, 1, ("qh",))):
                    for tt in range(ntt):
                        sq = sqp[tt % 2]
                        sl = slice(tt * 512, (tt + 1) * 512)
                        p.op("S", lambda e, src_b=src_b, sl=sl, base=base, sq=sq:
                             e.activation(sq[0:64, :], src_b[base:base + 64, sl],
                                          AF.Square),
                             reads=[rs], writes=[("atsq", tt % 2)])
                        p.op("T", lambda e, sq=sq: e.matmul(
                            env.ps[6][0:1, 0:512], env.ones_col_bf[0:64, :],
                            sq[0:64, :], start=True, stop=True,
                            tile_position=(0, 0)),
                            reads=[("atsq", tt % 2), ("ones_col_bf",)],
                            writes=[("ps", 6)])
                        if tt == 0:
                            p.op("V", lambda e, so=so: e.reduce_max(
                                nrm[0:1, NS + so:NS + so + 1],
                                env.ps[6][0:1, 0:512], AX.X),
                                reads=[("ps", 6)], writes=[("nmx", so)])
                        else:
                            p.op("V", lambda e: e.reduce_max(
                                nrm[0:1, NS + 4:NS + 5],
                                env.ps[6][0:1, 0:512], AX.X),
                                reads=[("ps", 6)], writes=[("nmt",)])
                            p.op("V", lambda e, so=so: e.tensor_max(
                                nrm[0:1, NS + so:NS + so + 1],
                                nrm[0:1, NS + so:NS + so + 1],
                                nrm[0:1, NS + 4:NS + 5]),
                                reads=[("nmt",), ("nmx", so)],
                                writes=[("nmx", so)])
                for so in (0, 1):
                    p.op("S", lambda e, so=so: e.activation(
                        nrm[0:1, NS + 2 + so:NS + 3 + so],
                        nrm[0:1, NS + so:NS + so + 1], AF.Sqrt),
                        reads=[("nmx", so)], writes=[("nsq", so)])
                p.op("V", lambda e: e.tensor_mul(
                    nrm[0:1, NS + 5:NS + 6], nrm[0:1, NS + 2:NS + 3],
                    nrm[0:1, NS + 3:NS + 4]),
                    reads=[("nsq", 0), ("nsq", 1)], writes=[("ncc",)])
                p.op("V", lambda e: e.tensor_scalar(
                    nrm[0:1, NS + 6:NS + 7], nrm[0:1, NS + 5:NS + 6],
                    -SCL, None, ALU.mult),
                    reads=[("ncc",)], writes=[("nbc",)])
                p.op("T", lambda e: e.matmul(env.ps[6][:, 0:1], env.ones_row_f[:, :],
                                             nrm[0:1, NS + 6:NS + 7],
                                             start=True, stop=True),
                     reads=[("nbc",), ("ones_row_f",)], writes=[("ps", 6)])
                p.op("V", lambda e, hh=hh: e.tensor_copy(negc[:, hh:hh + 1],
                                                         env.ps[6][:, 0:1]),
                     reads=[("ps", 6)], writes=[("negc", hh)])
                def norm_a0(qt, abk, qsl, base=base):
                    p.op("V", lambda e, abk=abk: e.reciprocal(
                        rscr[0:1, :], env.ps[abk][64:65, :]),
                        reads=[("ps", abk)], writes=[("nrec",)])
                    p.op("V", lambda e: e.tensor_copy(rrow[0:1, :],
                                                      rscr[0:1, :]),
                         reads=[("nrec",)], writes=[("nrecb",)])
                    p.op("T", lambda e, qt=qt: e.matmul(
                        env.ps[qt % 2][0:64, :], env.ones_row_bf[:, 0:64],
                        rrow[0:1, :], start=True, stop=True),
                        reads=[("nrecb",), ("ones_row_bf",)],
                        writes=[("ps", qt % 2)])
                    p.op("V", lambda e, qt=qt: e.tensor_copy(
                        rbt[qt % 2][0:64, :], env.ps[qt % 2][0:64, :]),
                        reads=[("ps", qt % 2)], writes=[("rbt", qt % 2)])
                    p.op("V", lambda e, qt=qt, qsl=qsl, abk=abk, base=base:
                         e.tensor_mul(oTp[base:base + 64, qsl],
                                      env.ps[abk][0:64, :],
                                      rbt[qt % 2][0:64, :]),
                         reads=[("ps", abk), ("rbt", qt % 2)],
                         writes=[("oTp",)])
                pend = None
                for qt in range(ntt):
                    qsl = slice(qt * 512, (qt + 1) * 512)
                    abk = 6 + qt % 2
                    voff = 0 if base == 0 else 65
                    nkc = min(ntc, 4 * qt + 4)
                    LA = 2
                    for step in range(nkc + LA):
                        if step == 4 and pend is not None:
                            norm_a0(*pend)
                            pend = None
                        if step < nkc:
                            kc = step
                            sbk = 2 + kc % 4
                            pt = ptb[kc % 4]
                            p.op("T", lambda e, kc=kc, qsl=qsl, base=base, sbk=sbk, tp=tp:
                                 e.matmul(env.ps[sbk][:, :],
                                          khb[base:base + 64,
                                              kc * 128:(kc + 1) * 128],
                                          qhb[base:base + 64, qsl],
                                          start=True, stop=True, tile_position=tp),
                                 reads=[("kh",), ("qh",)], writes=[("ps", sbk)])
                            p.op("S", lambda e, pt=pt, sbk=sbk, hh=hh: e.activation(
                                pt[:, :], env.ps[sbk][:, :], AF.Exp,
                                bias=negc[:, hh:hh + 1], scale=sclc[:, 0:1]),
                                reads=[("ps", sbk), ("negc", hh), ("sclc",)],
                                writes=[("pt", kc % 4)])
                            if kc >= 4 * qt:
                                p.op("G", lambda e, pt=pt, kc=kc, qt=qt: e.affine_select(
                                    out=pt[:, :], in_=pt[:, :], compare_op=ALU.is_ge,
                                    fill=0.0, base=512 * qt - 128 * kc,
                                    pattern=[[1, 512]], channel_multiplier=-1),
                                    reads=[("pt", kc % 4)], writes=[("pt", kc % 4)])
                        if step >= LA:
                            kc = step - LA
                            pt = ptb[kc % 4]
                            p.op("T", lambda e, pt=pt, kc=kc, voff=voff, abk=abk,
                                 nkc=nkc:
                                 e.matmul(env.ps[abk][0:65, :],
                                          vhb[:, kc * 130 + voff:kc * 130 + voff + 65],
                                          pt[:, :],
                                          start=(kc == 0), stop=(kc == nkc - 1)),
                                 reads=[("pt", kc % 4), ("vh", kc)],
                                 writes=[("ps", abk)])
                    pend = (qt, abk, qsl)
                norm_a0(*pend)
            for dc_o in range(nch):
                for tt in range(ntt):
                    bk = 4 + (dc_o * ntt + tt) % 2
                    sl = slice(tt * 512, (tt + 1) * 512)
                    sp = (dc_o * ntt + tt) % 2
                    p.op("T", lambda e, dc_o=dc_o, sl=sl, bk=bk, hp=hp: e.matmul(
                        env.ps[bk][:, :],
                        wo_sb[hp % 2][:, dc_o * 128:(dc_o + 1) * 128],
                        oTp[:, sl], start=True, stop=True),
                        reads=[("wo", hp % 2), ("oTp",)], writes=[("ps", bk)])
                    if hp == 0:
                        p.op("S", lambda e, dc_o=dc_o, bk=bk, sp=sp: e.activation(
                            fx[sp][:, :], env.ps[bk][:, :], AF.Identity,
                            bias=bo_sb[:, dc_o:dc_o + 1]),
                            reads=[("ps", bk), ("bo",)], writes=[("fx", sp)])
                        p.op("V", lambda e, dc_o=dc_o, sl=sl, sp=sp: e.tensor_add(
                            fx[sp][:, :], fx[sp][:, :],
                            A[:, dc_o * T:(dc_o + 1) * T][:, sl]),
                            reads=[("fx", sp), ("A", dc_o)], writes=[("fx", sp)])
                        p.dma("DG", xa1_h.rearrange("(c p) t -> p c t", p=128)
                              [:, dc_o:dc_o + 1, sl],
                              fx[sp][:, :].rearrange("p (c t) -> p c t", c=1),
                              reads=[("fx", sp)], writes=[("xa1H", dc_o, tt)],
                              lane=2 + sp)
                    else:
                        p.op("V", lambda e, bk=bk, sp=sp: e.tensor_copy(
                            fx[sp][:, :], env.ps[bk][:, :]),
                            reads=[("ps", bk)], writes=[("fx", sp)])
                        p.dma("DG", xa1_h.rearrange("(c p) t -> p c t", p=128)
                              [:, dc_o:dc_o + 1, sl],
                              fx[sp][:, :].rearrange("p (c t) -> p c t", c=1),
                              reads=[("fx", sp)], writes=[("xa1H", dc_o, tt)],
                              accum_op=ALU.add, lane=2 + sp)

        def pre2(tt):
            p.dma("DG", lnsrc[tt % 2][:, :].rearrange("p (c t) -> p c t", t=512),
                  xa1_h.rearrange("(c p) t -> p c t", p=128)
                  [:, :, tt * 512:(tt + 1) * 512],
                  reads=[("xa1H", dc, tt) for dc in range(nch)],
                  writes=[("st", tt % 2)])

        def post2(dc, tt):
            p.dma("DY", xh2_h.rearrange("(c p) t -> p c t", p=128)
                  [:, dc:dc + 1, tt * 512:(tt + 1) * 512],
                  ostg[dc % 2][:, :].rearrange("p (c t) -> p c t", c=1),
                  reads=[("ostg", dc % 2)], writes=[("xh2H", dc, tt)])
        emit_ln(env, T,
                src=lambda dc, tt: lnsrc[tt % 2][:, dc * 512:(dc + 1) * 512],
                dst=lambda dc, tt: ostg[dc % 2][:, :],
                rsrc=lambda dc, tt: ("st", tt % 2),
                rdst=lambda dc, tt: ("ostg", dc % 2),
                bufs=lnb, gamma=g2_sb, beta=be2_sb, rgam=[("g2",), ("be2",)],
                preload=pre2, post=post2)

        for piece in range(8):
            p.dma("DG", A[:, piece * 4 * D:(piece + 1) * 4 * D]
                  .rearrange("p (c n) -> p c n", n=D),
                  w2_h.rearrange("(c p) n -> p c n", p=128)
                  [:, piece * 4:(piece + 1) * 4, :],
                  writes=[("A", piece)], tag="w2load")

        p.wait_all("S")
        for tt in range(ntt):
            sl = slice(tt * 512, (tt + 1) * 512)
            p.dma("DY", xstr[tt % 2][:, :].rearrange("p (c t) -> p c t", t=512),
                  xh2_h.rearrange("(c p) t -> p c t", p=128)[:, :, sl],
                  reads=[("xh2H", dc, tt) for dc in range(nch)],
                  writes=[("st", tt % 2)])
            for fc in range(32):
                wb_ = w1s[fc % 2]
                p.dma("DG", wb_[:, :].rearrange("p (c n) -> p c n", n=128),
                      w1_h.rearrange("(c p) n -> p c n", p=128)
                      [:, :, fc * 128:(fc + 1) * 128],
                      writes=[("w1s", fc % 2)], tag="w1load")
                bk = 4 + fc % 2
                p.op("T", lambda e, fc=fc, bk=bk, tt=tt, wb_=wb_: [
                    e.matmul(env.ps[bk][:, :],
                             wb_[:, dc * 128:(dc + 1) * 128],
                             xstr[tt % 2][:, dc * 512:(dc + 1) * 512],
                             start=(dc == 0), stop=(dc == nch - 1))
                    for dc in range(nch)][-1],
                    reads=[("w1s", fc % 2), ("st", tt % 2)],
                    writes=[("ps", bk)])
                p.op("S", lambda e, fc=fc, bk=bk: e.activation(
                    Hslice(fc), env.ps[bk][:, :], AF.Gelu,
                    bias=b1_sb[:, fc:fc + 1]),
                    reads=[("ps", bk), ("b1",)], writes=[("H", fc)])
            for dc_o in range(nch):
                bk = dc_o % 4
                sp = dc_o % 2
                p.op("T", lambda e, dc_o=dc_o, bk=bk: [
                    e.matmul(env.ps[bk][:, :],
                             A[:, fc * D + dc_o * 128: fc * D + (dc_o + 1) * 128],
                             Hslice(fc), start=(fc == 0), stop=(fc == 31))
                    for fc in range(32)][-1],
                    reads=[("A", piece) for piece in range(8)]
                    + [("H", fc) for fc in range(32)],
                    writes=[("ps", bk)])
                p.op("S", lambda e, dc_o=dc_o, bk=bk, sp=sp: e.activation(
                    fx[sp][:, :], env.ps[bk][:, :], AF.Identity,
                    bias=b2_sb[:, dc_o:dc_o + 1]),
                    reads=[("ps", bk), ("b2",)], writes=[("fx", sp)])
                p.dma("DY", fr[sp][:, :].rearrange("p (c t) -> p c t", c=1),
                      xa1_h.rearrange("(c p) t -> p c t", p=128)
                      [:, dc_o:dc_o + 1, sl],
                      reads=[("xa1H", dc_o, tt)], writes=[("fr", sp)])
                p.op("V", lambda e, sp=sp: e.tensor_add(
                    fx[sp][:, :], fx[sp][:, :], fr[sp][:, :]),
                    reads=[("fx", sp), ("fr", sp)], writes=[("fx", sp)])
                p.dma("DY", fr[sp][:, :].rearrange("p (c t) -> p c t", c=1),
                      x_h.rearrange("(c p) t -> p c t", p=128)
                      [:, dc_o:dc_o + 1, sl],
                      reads=[], writes=[("fr", sp)])
                p.op("V", lambda e, sp=sp: e.tensor_add(
                    fx[sp][:, :], fx[sp][:, :], fr[sp][:, :]),
                    reads=[("fx", sp), ("fr", sp)], writes=[("fx", sp)])
                p.dma("DY", y_h.rearrange("(c p) t -> p c t", p=128)
                      [:, dc_o:dc_o + 1, sl],
                      fx[sp][:, :].rearrange("p (c t) -> p c t", c=1),
                      reads=[("fx", sp)], writes=[("yH", dc_o, tt)])
        if fused is None:
            p.wait_all("Y")
            block = ctx.enter_context(nc.Block())
            p.emit(block)
    return nc



def build_a1_program(T=4096, fused=None, pfx=""):
    nch = 8
    ntt = T // 512
    ntc = T // 128
    SCL = 0.125
    nc = fused["nc"] if fused else bass.Bass("TRN2", target_bir_lowering=False,
                                             debug=False)
    x_h = fused["x"] if fused else dram(nc, pfx + "x", [D, T], F32, "ExternalInput")
    wq_h = dram(nc, pfx + "wq", [2, D, D], F32, "ExternalInput")
    wk_h = dram(nc, pfx + "wk", [2, D, D], F32, "ExternalInput")
    wv_h = dram(nc, pfx + "wv", [2, D, D], F32, "ExternalInput")
    wo_h = dram(nc, pfx + "wo", [2, D, D], F32, "ExternalInput")
    bq_h = dram(nc, pfx + "bq2", [2, 128, 8], F32, "ExternalInput")
    bk_h = dram(nc, pfx + "bk2", [2, 128, 8], F32, "ExternalInput")
    bv_h = dram(nc, pfx + "bvr", [2, 1, D], F32, "ExternalInput")
    bo_h = dram(nc, pfx + "bo2", [2, 128, 8], F32, "ExternalInput")
    w1_h = dram(nc, pfx + "w1", [2, D, 4 * D], F32, "ExternalInput")
    b1_h = dram(nc, pfx + "b1_2", [2, 128, 32], F32, "ExternalInput")
    w2_h = dram(nc, pfx + "w2", [2, 4 * D, D], F32, "ExternalInput")
    b2_h = dram(nc, pfx + "b2_2", [2, 128, 8], F32, "ExternalInput")
    g1_h = dram(nc, pfx + "g1", [2, 128, 8], F32, "ExternalInput")
    be1_h = dram(nc, pfx + "be1", [2, 128, 8], F32, "ExternalInput")
    g2_h = dram(nc, pfx + "g2", [2, 128, 8], F32, "ExternalInput")
    be2_h = dram(nc, pfx + "be2", [2, 128, 8], F32, "ExternalInput")
    gf_h = dram(nc, pfx + "gf2", [128, 8], F32, "ExternalInput")
    bef_h = dram(nc, pfx + "bef2", [128, 8], F32, "ExternalInput")
    m3_h = dram(nc, pfx + "m3r", [1, 516], F32, "ExternalInput")
    y_h = fused["y"] if fused else dram(nc, pfx + "y", [D, T], F32, "ExternalOutput")
    xh1_h = dram(nc, pfx + "a1xh", [D, T], BF16, "Internal")
    xcA_h = dram(nc, pfx + "a1xcA", [D, T], F32, "Internal")
    xcB_h = dram(nc, pfx + "a1xcB", [D, T], F32, "Internal")

    with (ExitStack() if fused is None else _KeepOpen(fused["ctx"])) as ctx:
        if fused is None:
            p = Prog(nc, ctx)
            env = Env(nc, ctx, p)
            pool = Pool(env, POOL_COLS)
            p.ns = "s"
        else:
            p, env, pool = fused["p"], fused["env"], fused["pool"]
            pool.reset()
            p.ns = pfx
        p.barrier()
        A = pool.carve(max(nch * T, 32 * D), BF16, parts=128)
        U = pool.carve(16640, BF16, parts=128)
        qhb = U[:, 0:4096]
        khb = U[:, 4096:8192]
        vhb = U[:, 8192:12352]
        oTp = U[:, 12352:16448]
        Hslice = lambda fc: U[:, fc * 512:(fc + 1) * 512]
        lnsrc = [pool.carve(nch * 512, BF16, parts=128) for i in range(2)]
        xstr = lnsrc
        m3b = pool.carve(516, BF16, parts=128)
        m3row = pool.carve(516, BF16, parts=1)
        mask12 = pool.carve(12 * 512, BF16, parts=128)
        wo_sb = [pool.carve(D, BF16, parts=128) for i in range(2)]
        wsl = [[pool.carve(nch * 128, BF16, parts=128) for j in range(3)]
               for i in range(2)]
        w1s = [pool.carve(nch * 128, BF16, parts=128) for i in range(2)]
        qraw = [pool.carve(512, BF16, parts=128) for i in range(2)]
        ptb = [pool.carve(512, BF16, parts=128) for i in range(4)]
        sqp = [pool.carve(512, BF16, parts=128) for i in range(2)]
        nrm = pool.carve(512 + 16, F32, parts=1)
        negc = pool.carve(2, F32, parts=128)
        sclc = pool.carve(1, F32, parts=128)
        rscr = nrm[0:1, 0:512]
        rrow = pool.carve(512, BF16, parts=1)
        rbt = [pool.carve(512, BF16, parts=128) for i in range(2)]
        fx = [pool.carve(512, F32, parts=128) for i in range(2)]
        fr = [pool.carve(512, F32, parts=128) for i in range(2)]
        ostg = [pool.carve(512, BF16, parts=128) for i in range(2)]
        bq_sb = [pool.carve(8, F32, parts=128) for i in range(2)]
        bk_sb = [pool.carve(8, F32, parts=128) for i in range(2)]
        bv_sb = [pool.carve(D, BF16, parts=1) for i in range(2)]
        bo_sb = [pool.carve(8, F32, parts=128) for i in range(2)]
        b1_sb = [pool.carve(32, F32, parts=128) for i in range(2)]
        b2_sb = [pool.carve(8, F32, parts=128) for i in range(2)]
        g1_sb = [pool.carve(8, F32, parts=128) for i in range(2)]
        be1_sb = [pool.carve(8, F32, parts=128) for i in range(2)]
        g2_sb = [pool.carve(8, F32, parts=128) for i in range(2)]
        be2_sb = [pool.carve(8, F32, parts=128) for i in range(2)]
        gf_sb = pool.carve(8, F32, parts=128)
        bef_sb = pool.carve(8, F32, parts=128)
        lnb = pool_ln_bufs(pool)
        NS = 512

        p.op("G", lambda e: e.memset(sclc[:, :], SCL), writes=[("sclc",)])
        p.dma("DG", m3row[:, :], m3_h, writes=[("m3r",)])
        for i in range(2):
            w = min(512, 516 - i * 512)
            p.op("T", lambda e, i=i, w=w: e.matmul(
                env.ps[6][:, 0:w], env.ones_row_bf[:, :],
                m3row[0:1, i * 512:i * 512 + w], start=True, stop=True),
                reads=[("m3r",), ("ones_row_bf",)], writes=[("ps", 6)])
            p.op("V", lambda e, i=i, w=w: e.tensor_copy(
                m3b[:, i * 512:i * 512 + w], env.ps[6][:, 0:w]),
                reads=[("ps", 6)], writes=[("m3b",)])
        for j in range(4):
            for ph in range(3):
                mi = j * 3 + ph
                p.op("G", lambda e, j=j, ph=ph, mi=mi: e.affine_select(
                    out=mask12[:, mi * 512:(mi + 1) * 512],
                    in_=m3b[:, ph:ph + 512], compare_op=ALU.is_gt, fill=1.0,
                    base=128 * j, pattern=[[-1, 512]], channel_multiplier=1),
                    reads=[("m3b",)], writes=[("m12", mi)])
        for li in range(2):
            for t_, h_, r_ in ((bq_sb[li], bq_h[li], ("bq", li)),
                               (bk_sb[li], bk_h[li], ("bk", li)),
                               (bo_sb[li], bo_h[li], ("bo", li)),
                               (b1_sb[li], b1_h[li], ("b1", li)),
                               (b2_sb[li], b2_h[li], ("b2", li)),
                               (g1_sb[li], g1_h[li], ("g1", li)),
                               (be1_sb[li], be1_h[li], ("be1", li)),
                               (g2_sb[li], g2_h[li], ("g2", li)),
                               (be2_sb[li], be2_h[li], ("be2", li))):
                p.dma("DY", t_[:, :], h_, writes=[r_])
            p.dma("DG", bv_sb[li][:, :], bv_h[li], writes=[("bv", li)])
        p.dma("DY", gf_sb[:, :], gf_h, writes=[("gf",)])
        p.dma("DY", bef_sb[:, :], bef_h, writes=[("bef",)])

        def pre0(tt):
            p.dma("DG", lnsrc[tt % 2][:, :].rearrange("p (c t) -> p c t", t=512),
                  x_h.rearrange("(c p) t -> p c t", p=128)
                  [:, :, tt * 512:(tt + 1) * 512],
                  writes=[("st", tt % 2)])
        emit_ln(env, T,
                src=lambda dc, tt: lnsrc[tt % 2][:, dc * 512:(dc + 1) * 512],
                dst=lambda dc, tt: A[:, dc * T + tt * 512: dc * T + (tt + 1) * 512],
                rsrc=lambda dc, tt: ("st", tt % 2),
                rdst=lambda dc, tt: ("A", dc),
                bufs=lnb, preload=pre0)

        xc_h = [xcA_h, xcB_h]
        for li in range(2):
            xa_out = xc_h[0]
            xf_out = xc_h[1]
            def post1(dc, tt):
                p.dma("DY", xh1_h.rearrange("(c p) t -> p c t", p=128)
                      [:, dc:dc + 1, tt * 512:(tt + 1) * 512],
                      ostg[dc % 2][:, :].rearrange("p (c t) -> p c t", c=1),
                      reads=[("ostg", dc % 2)], writes=[("xh1H", dc, tt)])
            emit_ln(env, T,
                    src=lambda dc, tt: A[:, dc * T + tt * 512:
                                         dc * T + (tt + 1) * 512],
                    dst=lambda dc, tt: ostg[dc % 2][:, :],
                    rsrc=lambda dc, tt: ("A", dc),
                    rdst=lambda dc, tt: ("ostg", dc % 2),
                    bufs=lnb, gamma=g1_sb[li], beta=be1_sb[li],
                    rgam=[("g1", li), ("be1", li)], post=post1)

            p.op("G", lambda e: e.memset(vhb[:, :], 1.0),
                 writes=[("vh", tc) for tc in range(ntc)])
            for hp in range(nch):
                wp = wsl[hp % 2]
                p.dma("DG", wo_sb[hp % 2][:, :].rearrange("p (c n) -> p c n", c=1),
                      wo_h[li].rearrange("(c p) n -> p c n", p=128)[:, hp:hp + 1, :],
                      writes=[("wo", hp % 2)], tag=f"wo{li}_{hp}", lane=0)
                for j, wsrc in enumerate((wq_h[li], wk_h[li], wv_h[li])):
                    p.dma("DG", wp[j][:, :].rearrange("p (c n) -> p c n", n=128),
                          wsrc.rearrange("(c p) n -> p c n", p=128)
                          [:, :, hp * 128:(hp + 1) * 128],
                          writes=[("wsl", hp % 2, j)], tag=f"w{li}_{hp}", lane=1)
                for tt in range(ntt):
                    sl = slice(tt * 512, (tt + 1) * 512)
                    p.dma("DY" if tt % 2 == 0 else "DA",
                          xstr[tt % 2][:, :]
                          .rearrange("p (c t) -> p c t", t=512),
                          xh1_h.rearrange("(c p) t -> p c t", p=128)[:, :, sl],
                          reads=[("xh1H", dc, tt) for dc in range(nch)],
                          writes=[("st", tt % 2)])
                    for j, dstb, rd, bsb in ((0, qhb, ("qh",), bq_sb[li]),
                                             (1, khb, ("kh",), bk_sb[li])):
                        bk = 4 + j
                        p.op("T", lambda e, j=j, bk=bk, tt=tt, wp=wp: [
                            e.matmul(env.ps[bk][:, :],
                                     wp[j][:, dc * 128:(dc + 1) * 128],
                                     xstr[tt % 2][:, dc * 512:(dc + 1) * 512],
                                     start=(dc == 0), stop=(dc == nch - 1))
                            for dc in range(nch)][-1],
                            reads=[("wsl", hp % 2, j), ("st", tt % 2)],
                            writes=[("ps", bk)])
                        p.op("S", lambda e, bk=bk, dstb=dstb, sl=sl, hp=hp,
                             bsb=bsb: e.activation(
                                 dstb[:, sl], env.ps[bk][:, :], AF.Identity,
                                 bias=bsb[:, hp:hp + 1]),
                             reads=[("ps", bk), ("bq", li), ("bk", li)],
                             writes=[rd])
                    for tq in range(4):
                        tc = tt * 4 + tq
                        bk = 6 + tc % 2
                        p.op("T", lambda e, tc=tc, tq=tq, bk=bk, tt=tt, wp=wp,
                             hp=hp, li=li: [
                            e.matmul(env.ps[bk][:, 0:128],
                                     xstr[tt % 2][:, dc * 512 + tq * 128:
                                                  dc * 512 + (tq + 1) * 128],
                                     wp[2][:, dc * 128:(dc + 1) * 128],
                                     start=(dc == 0), stop=False)
                            for dc in range(nch)] and
                            e.matmul(env.ps[bk][:, 0:128], env.ones_row_bf[:, :],
                                     bv_sb[li][0:1, hp * 128:(hp + 1) * 128],
                                     start=False, stop=True),
                            reads=[("wsl", hp % 2, 2), ("st", tt % 2),
                                   ("bv", li), ("ones_row_bf",)],
                            writes=[("ps", bk)])
                        p.op("V", lambda e, tc=tc, bk=bk: [
                            e.tensor_copy(vhb[:, tc * 130:tc * 130 + 64],
                                          env.ps[bk][:, 0:64]),
                            e.tensor_copy(vhb[:, tc * 130 + 65:tc * 130 + 129],
                                          env.ps[bk][:, 64:128])][-1],
                            reads=[("ps", bk)], writes=[("vh", tc)])
                for hh in range(2):
                    base = hh * 64
                    tp = (base, 0)
                    for src_b, so, rs in ((khb, 0, ("kh",)), (qhb, 1, ("qh",))):
                        for tt in range(ntt):
                            sq = sqp[tt % 2]
                            sl = slice(tt * 512, (tt + 1) * 512)
                            p.op("S", lambda e, src_b=src_b, sl=sl, base=base,
                                 sq=sq: e.activation(
                                     sq[0:64, :], src_b[base:base + 64, sl],
                                     AF.Square),
                                 reads=[rs], writes=[("atsq", tt % 2)])
                            p.op("T", lambda e, sq=sq: e.matmul(
                                env.ps[6][0:1, 0:512], env.ones_col_bf[0:64, :],
                                sq[0:64, :], start=True, stop=True,
                                tile_position=(0, 0)),
                                reads=[("atsq", tt % 2), ("ones_col_bf",)],
                                writes=[("ps", 6)])
                            if tt == 0:
                                p.op("V", lambda e, so=so: e.reduce_max(
                                    nrm[0:1, NS + so:NS + so + 1],
                                    env.ps[6][0:1, 0:512], AX.X),
                                    reads=[("ps", 6)], writes=[("nmx", so)])
                            else:
                                p.op("V", lambda e: e.reduce_max(
                                    nrm[0:1, NS + 4:NS + 5],
                                    env.ps[6][0:1, 0:512], AX.X),
                                    reads=[("ps", 6)], writes=[("nmt",)])
                                p.op("V", lambda e, so=so: e.tensor_max(
                                    nrm[0:1, NS + so:NS + so + 1],
                                    nrm[0:1, NS + so:NS + so + 1],
                                    nrm[0:1, NS + 4:NS + 5]),
                                    reads=[("nmt",), ("nmx", so)],
                                    writes=[("nmx", so)])
                    for so in (0, 1):
                        p.op("S", lambda e, so=so: e.activation(
                            nrm[0:1, NS + 2 + so:NS + 3 + so],
                            nrm[0:1, NS + so:NS + so + 1], AF.Sqrt),
                            reads=[("nmx", so)], writes=[("nsq", so)])
                    p.op("V", lambda e: e.tensor_mul(
                        nrm[0:1, NS + 5:NS + 6], nrm[0:1, NS + 2:NS + 3],
                        nrm[0:1, NS + 3:NS + 4]),
                        reads=[("nsq", 0), ("nsq", 1)], writes=[("ncc",)])
                    p.op("V", lambda e: e.tensor_scalar(
                        nrm[0:1, NS + 6:NS + 7], nrm[0:1, NS + 5:NS + 6],
                        -SCL, None, ALU.mult),
                        reads=[("ncc",)], writes=[("nbc",)])
                    p.op("T", lambda e: e.matmul(
                        env.ps[6][:, 0:1], env.ones_row_f[:, :],
                        nrm[0:1, NS + 6:NS + 7], start=True, stop=True),
                        reads=[("nbc",), ("ones_row_f",)], writes=[("ps", 6)])
                    p.op("V", lambda e, hh=hh: e.tensor_copy(
                        negc[:, hh:hh + 1], env.ps[6][:, 0:1]),
                        reads=[("ps", 6)], writes=[("negc", hh)])
                    def norm_a1(qt, abk, qsl, base=base):
                        p.op("V", lambda e, abk=abk: e.reciprocal(
                            rscr[0:1, :], env.ps[abk][64:65, :]),
                            reads=[("ps", abk)], writes=[("nrec",)])
                        p.op("V", lambda e: e.tensor_copy(rrow[0:1, :],
                                                          rscr[0:1, :]),
                             reads=[("nrec",)], writes=[("nrecb",)])
                        p.op("T", lambda e, qt=qt: e.matmul(
                            env.ps[qt % 2][0:64, :],
                            env.ones_row_bf[:, 0:64],
                            rrow[0:1, :], start=True, stop=True),
                            reads=[("nrecb",), ("ones_row_bf",)],
                            writes=[("ps", qt % 2)])
                        p.op("V", lambda e, qt=qt: e.tensor_copy(
                            rbt[qt % 2][0:64, :], env.ps[qt % 2][0:64, :]),
                            reads=[("ps", qt % 2)], writes=[("rbt", qt % 2)])
                        p.op("V", lambda e, qt=qt, qsl=qsl, abk=abk, base=base:
                             e.tensor_mul(oTp[base:base + 64, qsl],
                                          env.ps[abk][0:64, :],
                                          rbt[qt % 2][0:64, :]),
                             reads=[("ps", abk), ("rbt", qt % 2)],
                             writes=[("oTp",)])
                    pend = None
                    for qt in range(ntt):
                        qsl = slice(qt * 512, (qt + 1) * 512)
                        abk = 6 + qt % 2
                        voff = 0 if base == 0 else 65
                        ph = (512 * qt) % 3
                        LA = 2
                        for step in range(ntc + LA):
                            if step == 4 and pend is not None:
                                norm_a1(*pend)
                                pend = None
                            if step < ntc:
                                kc = step
                                sbk = 2 + kc % 4
                                pt = ptb[kc % 4]
                                p.op("T", lambda e, kc=kc, qsl=qsl, base=base,
                                     sbk=sbk, tp=tp: e.matmul(
                                         env.ps[sbk][:, :],
                                         khb[base:base + 64,
                                             kc * 128:(kc + 1) * 128],
                                         qhb[base:base + 64, qsl],
                                         start=True, stop=True,
                                         tile_position=tp),
                                     reads=[("kh",), ("qh",)],
                                     writes=[("ps", sbk)])
                                p.op("S", lambda e, pt=pt, sbk=sbk, hh=hh:
                                     e.activation(
                                         pt[:, :], env.ps[sbk][:, :], AF.Exp,
                                         bias=negc[:, hh:hh + 1],
                                         scale=sclc[:, 0:1]),
                                     reads=[("ps", sbk), ("negc", hh),
                                            ("sclc",)],
                                     writes=[("pt", kc % 4)])
                                if kc >= 4 * qt + 4:
                                    p.op("V", lambda e, pt=pt, ph=ph:
                                         e.tensor_mul(pt[:, :], pt[:, :],
                                                      m3b[:, ph:ph + 512]),
                                         reads=[("pt", kc % 4), ("m3b",)],
                                         writes=[("pt", kc % 4)])
                                elif kc >= 4 * qt:
                                    mi = (kc - 4 * qt) * 3 + ph
                                    p.op("V", lambda e, pt=pt, mi=mi:
                                         e.tensor_mul(pt[:, :], pt[:, :],
                                                      mask12[:, mi * 512:
                                                             (mi + 1) * 512]),
                                         reads=[("pt", kc % 4), ("m12", mi)],
                                         writes=[("pt", kc % 4)])
                            if step >= LA:
                                kc = step - LA
                                pt = ptb[kc % 4]
                                p.op("T", lambda e, pt=pt, kc=kc, voff=voff,
                                     abk=abk: e.matmul(
                                         env.ps[abk][0:65, :],
                                         vhb[:, kc * 130 + voff:
                                             kc * 130 + voff + 65],
                                         pt[:, :],
                                         start=(kc == 0),
                                         stop=(kc == ntc - 1)),
                                     reads=[("pt", kc % 4), ("vh", kc)],
                                     writes=[("ps", abk)])
                        pend = (qt, abk, qsl)
                    norm_a1(*pend)
                for dc_o in range(nch):
                    for tt in range(ntt):
                        bk = 4 + (dc_o * ntt + tt) % 2
                        sl = slice(tt * 512, (tt + 1) * 512)
                        sp = (dc_o * ntt + tt) % 2
                        if hh == 0:
                            continue
                        p.op("T", lambda e, dc_o=dc_o, sl=sl, bk=bk, hp=hp:
                             e.matmul(
                                 env.ps[bk][:, :],
                                 wo_sb[hp % 2][:, dc_o * 128:(dc_o + 1) * 128],
                                 oTp[:, sl], start=True, stop=True),
                             reads=[("wo", hp % 2), ("oTp",)],
                             writes=[("ps", bk)])
                        if hp == 0:
                            p.op("S", lambda e, dc_o=dc_o, bk=bk, sp=sp, li=li:
                                 e.activation(
                                     fx[sp][:, :], env.ps[bk][:, :],
                                     AF.Identity,
                                     bias=bo_sb[li][:, dc_o:dc_o + 1]),
                                 reads=[("ps", bk), ("bo", li)],
                                 writes=[("fx", sp)])
                            p.op("V", lambda e, dc_o=dc_o, sl=sl, sp=sp:
                                 e.tensor_add(
                                     fx[sp][:, :], fx[sp][:, :],
                                     A[:, dc_o * T:(dc_o + 1) * T][:, sl]),
                                 reads=[("fx", sp), ("A", dc_o)],
                                 writes=[("fx", sp)])
                            p.dma("DG", xa_out.rearrange("(c p) t -> p c t",
                                                         p=128)
                                  [:, dc_o:dc_o + 1, sl],
                                  fx[sp][:, :].rearrange("p (c t) -> p c t",
                                                         c=1),
                                  reads=[("fx", sp)],
                                  writes=[("xaH", li, dc_o, tt)], lane=2 + sp)
                        else:
                            p.op("V", lambda e, bk=bk, sp=sp: e.tensor_copy(
                                fx[sp][:, :], env.ps[bk][:, :]),
                                reads=[("ps", bk)], writes=[("fx", sp)])
                            p.dma("DG", xa_out.rearrange("(c p) t -> p c t",
                                                         p=128)
                                  [:, dc_o:dc_o + 1, sl],
                                  fx[sp][:, :].rearrange("p (c t) -> p c t",
                                                         c=1),
                                  reads=[("fx", sp)],
                                  writes=[("xaH", li, dc_o, tt)],
                                  accum_op=ALU.add, lane=2 + sp)

            def pre2(tt, li=li, xa_out=xa_out):
                p.dma("DG", lnsrc[tt % 2][:, :]
                      .rearrange("p (c t) -> p c t", t=512),
                      xa_out.rearrange("(c p) t -> p c t", p=128)
                      [:, :, tt * 512:(tt + 1) * 512],
                      reads=[("xaH", li, dc, tt) for dc in range(nch)],
                      writes=[("st", tt % 2)])

            def post2(dc, tt, li=li):
                p.dma("DY", xh1_h.rearrange("(c p) t -> p c t", p=128)
                      [:, dc:dc + 1, tt * 512:(tt + 1) * 512],
                      ostg[dc % 2][:, :].rearrange("p (c t) -> p c t", c=1),
                      reads=[("ostg", dc % 2)], writes=[("xh2H", li, dc, tt)])
            emit_ln(env, T,
                    src=lambda dc, tt: lnsrc[tt % 2][:, dc * 512:(dc + 1) * 512],
                    dst=lambda dc, tt: ostg[dc % 2][:, :],
                    rsrc=lambda dc, tt: ("st", tt % 2),
                    rdst=lambda dc, tt: ("ostg", dc % 2),
                    bufs=lnb, gamma=g2_sb[li], beta=be2_sb[li],
                    rgam=[("g2", li), ("be2", li)], preload=pre2, post=post2)

            for piece in range(8):
                p.dma("DG", A[:, piece * 4 * D:(piece + 1) * 4 * D]
                      .rearrange("p (c n) -> p c n", n=D),
                      w2_h[li].rearrange("(c p) n -> p c n", p=128)
                      [:, piece * 4:(piece + 1) * 4, :],
                      writes=[("A", piece)], tag=f"w2load{li}")
            p.wait_all("S")
            for tt in range(ntt):
                sl = slice(tt * 512, (tt + 1) * 512)
                p.dma("DY", xstr[tt % 2][:, :]
                      .rearrange("p (c t) -> p c t", t=512),
                      xh1_h.rearrange("(c p) t -> p c t", p=128)[:, :, sl],
                      reads=[("xh2H", li, dc, tt) for dc in range(nch)],
                      writes=[("st", tt % 2)])
                for fc in range(32):
                    wb_ = w1s[fc % 2]
                    p.dma("DG", wb_[:, :].rearrange("p (c n) -> p c n", n=128),
                          w1_h[li].rearrange("(c p) n -> p c n", p=128)
                          [:, :, fc * 128:(fc + 1) * 128],
                          writes=[("w1s", fc % 2)], tag=f"w1load{li}")
                    bk = 4 + fc % 2
                    p.op("T", lambda e, fc=fc, bk=bk, tt=tt, wb_=wb_: [
                        e.matmul(env.ps[bk][:, :],
                                 wb_[:, dc * 128:(dc + 1) * 128],
                                 xstr[tt % 2][:, dc * 512:(dc + 1) * 512],
                                 start=(dc == 0), stop=(dc == nch - 1))
                        for dc in range(nch)][-1],
                        reads=[("w1s", fc % 2), ("st", tt % 2)],
                        writes=[("ps", bk)])
                    p.op("S", lambda e, fc=fc, bk=bk, li=li: e.activation(
                        Hslice(fc), env.ps[bk][:, :], AF.Gelu,
                        bias=b1_sb[li][:, fc:fc + 1]),
                        reads=[("ps", bk), ("b1", li)], writes=[("H", fc)])
                for dc_o in range(nch):
                    bk = dc_o % 4
                    sp = dc_o % 2
                    p.op("T", lambda e, dc_o=dc_o, bk=bk: [
                        e.matmul(env.ps[bk][:, :],
                                 A[:, fc * D + dc_o * 128:
                                     fc * D + (dc_o + 1) * 128],
                                 Hslice(fc), start=(fc == 0), stop=(fc == 31))
                        for fc in range(32)][-1],
                        reads=[("A", piece) for piece in range(8)]
                        + [("H", fc) for fc in range(32)],
                        writes=[("ps", bk)])
                    p.op("S", lambda e, dc_o=dc_o, bk=bk, sp=sp, li=li:
                         e.activation(
                             fx[sp][:, :], env.ps[bk][:, :], AF.Identity,
                             bias=b2_sb[li][:, dc_o:dc_o + 1]),
                         reads=[("ps", bk), ("b2", li)], writes=[("fx", sp)])
                    p.dma("DY", fr[sp][:, :].rearrange("p (c t) -> p c t", c=1),
                          xa_out.rearrange("(c p) t -> p c t", p=128)
                          [:, dc_o:dc_o + 1, sl],
                          reads=[("xaH", li, dc_o, tt)], writes=[("fr", sp)])
                    p.op("V", lambda e, sp=sp: e.tensor_add(
                        fx[sp][:, :], fx[sp][:, :], fr[sp][:, :]),
                        reads=[("fx", sp), ("fr", sp)], writes=[("fx", sp)])
                    p.dma("DG", xf_out.rearrange("(c p) t -> p c t", p=128)
                          [:, dc_o:dc_o + 1, sl],
                          fx[sp][:, :].rearrange("p (c t) -> p c t", c=1),
                          reads=[("fx", sp)], writes=[("xfH", li, dc_o, tt)])
            if li == 0:
                for tt in range(ntt):
                    p.dma("DG", lnsrc[tt % 2][:, :]
                          .rearrange("p (c t) -> p c t", t=512),
                          xf_out.rearrange("(c p) t -> p c t", p=128)
                          [:, :, tt * 512:(tt + 1) * 512],
                          reads=[("xfH", li, dc, tt) for dc in range(nch)],
                          writes=[("st", tt % 2)])
                    for dc in range(nch):
                        p.op("V", lambda e, dc=dc, tt=tt: e.tensor_copy(
                            A[:, dc * T + tt * 512: dc * T + (tt + 1) * 512],
                            lnsrc[tt % 2][:, dc * 512:(dc + 1) * 512]),
                            reads=[("st", tt % 2)], writes=[("A", dc)])

        def pre3(tt):
            p.dma("DG", lnsrc[tt % 2][:, :].rearrange("p (c t) -> p c t", t=512),
                  xc_h[1].rearrange("(c p) t -> p c t", p=128)
                  [:, :, tt * 512:(tt + 1) * 512],
                  reads=[("xfH", 1, dc, tt) for dc in range(nch)],
                  writes=[("st", tt % 2)])

        def post3(dc, tt):
            sp = dc % 2
            p.dma("DY", fr[sp][:, :].rearrange("p (c t) -> p c t", c=1),
                  x_h.rearrange("(c p) t -> p c t", p=128)
                  [:, dc:dc + 1, tt * 512:(tt + 1) * 512],
                  reads=[("fx", sp)], writes=[("fr", sp)])
            p.op("V", lambda e, sp=sp: e.tensor_add(
                fx[sp][:, :], fx[sp][:, :], fr[sp][:, :]),
                reads=[("fx", sp), ("fr", sp)], writes=[("fx", sp)])
            p.dma("DY", y_h.rearrange("(c p) t -> p c t", p=128)
                  [:, dc:dc + 1, tt * 512:(tt + 1) * 512],
                  fx[sp][:, :].rearrange("p (c t) -> p c t", c=1),
                  reads=[("fx", sp)], writes=[("yH", dc, tt)])
        emit_ln(env, T,
                src=lambda dc, tt: lnsrc[tt % 2][:, dc * 512:(dc + 1) * 512],
                dst=lambda dc, tt: fx[dc % 2][:, :],
                rsrc=lambda dc, tt: ("st", tt % 2),
                rdst=lambda dc, tt: ("fx", dc % 2),
                bufs=lnb, gamma=gf_sb, beta=bef_sb,
                rgam=[("gf",), ("bef",)], preload=pre3, post=post3)
        if fused is None:
            p.wait_all("Y")
            block = ctx.enter_context(nc.Block())
            p.emit(block)
    return nc



def build_f1_program(T=4096, dbg=False, fused=None, pfx=""):
    nch = 8
    ntt = T // 512
    ntc = T // 128
    C = T // 2
    SCL = 0.125
    nc = fused["nc"] if fused else bass.Bass("TRN2", target_bir_lowering=False,
                                             debug=False)
    x_h = fused["x"] if fused else dram(nc, pfx + "x", [D, T], F32, "ExternalInput")
    wg_h = dram(nc, pfx + "wg2", [128, 8], F32, "ExternalInput")
    wq_h = dram(nc, pfx + "wq", [D, D], F32, "ExternalInput")
    wk_h = dram(nc, pfx + "wk", [D, D], F32, "ExternalInput")
    wv_h = dram(nc, pfx + "wv", [D, D], F32, "ExternalInput")
    wo_h = dram(nc, pfx + "wo", [D, D], F32, "ExternalInput")
    iot_h = dram(nc, pfx + "iotac", [128, 1], F32, "ExternalInput")
    y_h = fused["y"] if fused else dram(nc, pfx + "y", [D, T], F32, "ExternalOutput")
    xh1_h = dram(nc, pfx + "f1xh", [D, T], BF16, "Internal")
    oacc_h = dram(nc, pfx + "f1oacc", [D, T], F32, "Internal")

    with (ExitStack() if fused is None else _KeepOpen(fused["ctx"])) as ctx:
        if fused is None:
            p = Prog(nc, ctx)
            env = Env(nc, ctx, p)
            pool = Pool(env, POOL_COLS)
            p.ns = "s"
        else:
            p, env, pool = fused["p"], fused["env"], fused["pool"]
            pool.reset()
            p.ns = pfx
        p.barrier()
        U = pool.carve(16640, BF16, parts=128)
        qhb = U[:, 0:4096]
        khb = U[:, 4096:8192]
        vhb = U[:, 8192:12352]
        oTp = U[:, 12352:16448]
        lnsrc = [pool.carve(nch * 512, BF16, parts=128) for i in range(2)]
        xstr = lnsrc
        gbc = pool.carve(T, F32, parts=128)
        wgs = pool.carve(nch, BF16, parts=128)
        wo_sb = [pool.carve(D, BF16, parts=128) for i in range(2)]
        wsl = [[pool.carve(nch * 128, BF16, parts=128) for j in range(3)]
               for i in range(2)]
        qraw = [pool.carve(512, BF16, parts=128) for i in range(2)]
        ptb = [pool.carve(512, BF16, parts=128) for i in range(4)]
        sqp = [pool.carve(512, BF16, parts=128) for i in range(2)]
        nrm = pool.carve(512 + 16, F32, parts=1)
        negc = pool.carve(2, F32, parts=128)
        sclc = pool.carve(1, F32, parts=128)
        rscr = nrm[0:1, 0:512]
        rrow = pool.carve(512, BF16, parts=1)
        rbt = [pool.carve(512, BF16, parts=128) for i in range(2)]
        fx = [pool.carve(512, F32, parts=128) for i in range(2)]
        fr = [pool.carve(512, F32, parts=128) for i in range(2)]
        ostg = [pool.carve(512, BF16, parts=128) for i in range(2)]
        grow = pool.carve(512, F32, parts=1)
        selb = pool.carve(1, BF16, parts=128)
        bch = pool.carve(64, F32, parts=128)
        ic1 = pool.carve(1, F32, parts=128)
        bis = pool.carve(16, F32, parts=1)
        p.op("G", lambda e: e.memset(bis[0:1, :], 0.0), writes=[("blo",), ("bhi",), ("bd",), ("bs",), ("bsd",)])
        bcol = pool.carve(8, F32, parts=128)
        cnt = pool.carve(2, F32, parts=128)
        ind = pool.carve(T // 8, BF16, parts=128)
        gtm = pool.carve(96, F32, parts=128)
        gmb = [pool.carve(512, F32, parts=128) for i in range(2)]
        lnb = pool_ln_bufs(pool)
        NS = 512

        p.op("G", lambda e: e.memset(sclc[:, :], SCL), writes=[("sclc",)])
        p.dma("DY", ic1[:, :], iot_h, writes=[("ic1",)])
        p.dma("DG", wgs[:, :], wg_h, writes=[("wg",)])

        def pre0(tt):
            p.dma("DG", lnsrc[tt % 2][:, :].rearrange("p (c t) -> p c t", t=512),
                  x_h.rearrange("(c p) t -> p c t", p=128)
                  [:, :, tt * 512:(tt + 1) * 512],
                  writes=[("st", tt % 2)])

        def post0(dc, tt):
            p.dma("DY", xh1_h.rearrange("(c p) t -> p c t", p=128)
                  [:, dc:dc + 1, tt * 512:(tt + 1) * 512],
                  ostg[dc % 2][:, :].rearrange("p (c t) -> p c t", c=1),
                  reads=[("ostg", dc % 2)], writes=[("xh1H", dc, tt)])
        emit_ln(env, T,
                src=lambda dc, tt: lnsrc[tt % 2][:, dc * 512:(dc + 1) * 512],
                dst=lambda dc, tt: ostg[dc % 2][:, :],
                rsrc=lambda dc, tt: ("st", tt % 2),
                rdst=lambda dc, tt: ("ostg", dc % 2),
                bufs=lnb, preload=pre0, post=post0)

        for tt in range(ntt):
            sl = slice(tt * 512, (tt + 1) * 512)
            p.dma("DY", xstr[tt % 2][:, :].rearrange("p (c t) -> p c t", t=512),
                  xh1_h.rearrange("(c p) t -> p c t", p=128)[:, :, sl],
                  reads=[("xh1H", dc, tt) for dc in range(nch)],
                  writes=[("st", tt % 2)])
            p.op("T", lambda e, tt=tt: [
                e.matmul(env.ps[6][0:1, :], wgs[:, dc:dc + 1],
                         xstr[tt % 2][:, dc * 512:(dc + 1) * 512],
                         start=(dc == 0), stop=(dc == nch - 1))
                for dc in range(nch)][-1],
                reads=[("wg",), ("st", tt % 2)], writes=[("ps", 6)])
            p.op("S", lambda e: e.activation(grow[0:1, :], env.ps[6][0:1, :],
                                             AF.Sigmoid),
                 reads=[("ps", 6)], writes=[("grow",)])
            p.op("T", lambda e: e.matmul(env.ps[7][:, :], env.ones_row_f[:, :],
                                         grow[0:1, :], start=True, stop=True),
                 reads=[("grow",), ("ones_row_f",)], writes=[("ps", 7)])
            p.op("V", lambda e, sl=sl: e.tensor_copy(gbc[:, sl], env.ps[7][:, :]),
                 reads=[("ps", 7)], writes=[("gbc", tt)])

        p.op("G", lambda e: e.memset(bis[0:1, 0:1], 0.0), writes=[("blo",)])
        p.op("G", lambda e: e.memset(bis[0:1, 1:2], 1.0), writes=[("bhi",)])
        gbr = [("gbc", tt) for tt in range(ntt)]
        for rnd in range(5):
            p.op("V", lambda e: e.tensor_sub(bis[0:1, 2:3], bis[0:1, 1:2],
                                             bis[0:1, 0:1]),
                 reads=[("blo",), ("bhi",)], writes=[("bd",)])
            p.op("V", lambda e: e.tensor_scalar(bis[0:1, 2:3], bis[0:1, 2:3],
                                                1.0 / 129.0, None, ALU.mult),
                 reads=[("bd",)], writes=[("bd",)])
            p.op("T", lambda e: e.matmul(env.ps[6][:, 0:1], env.ones_row_f[:, :],
                                         bis[0:1, 0:1], start=True, stop=True),
                 reads=[("blo",), ("ones_row_f",)], writes=[("ps", 6)])
            p.op("V", lambda e: e.tensor_copy(bcol[:, 1:2], env.ps[6][:, 0:1]),
                 reads=[("ps", 6)], writes=[("blob",)])
            p.op("T", lambda e: e.matmul(env.ps[6][:, 0:1], env.ones_row_f[:, :],
                                         bis[0:1, 2:3], start=True, stop=True),
                 reads=[("bd",), ("ones_row_f",)], writes=[("ps", 6)])
            p.op("V", lambda e: e.tensor_copy(bcol[:, 2:3], env.ps[6][:, 0:1]),
                 reads=[("ps", 6)], writes=[("bdb",)])
            p.op("V", lambda e: e.tensor_mul(bcol[:, 0:1], ic1[:, :],
                                             bcol[:, 2:3]),
                 reads=[("ic1",), ("bdb",)], writes=[("bth",)])
            p.op("V", lambda e: e.tensor_add(bcol[:, 0:1], bcol[:, 0:1],
                                             bcol[:, 1:2]),
                 reads=[("bth",), ("blob",)], writes=[("bth",)])
            for i8 in range(8):
                w = T // 8
                p.op("V", lambda e, i8=i8, w=w: e.tensor_scalar(
                    ind[:, 0:w], gbc[:, i8 * w:(i8 + 1) * w], bcol[:, 0:1],
                    None, ALU.is_ge),
                    reads=gbr + [("bth",)], writes=[("bind",)])
                p.op("V", lambda e, i8=i8, w=w: e.reduce_sum(
                    cnt[:, 1:2], ind[:, 0:w], AX.X),
                    reads=[("bind",)], writes=[("bcnt1",)])
                if i8 == 0:
                    p.op("V", lambda e: e.tensor_copy(cnt[:, 0:1], cnt[:, 1:2]),
                         reads=[("bcnt1",)], writes=[("bcnt",)])
                else:
                    p.op("V", lambda e: e.tensor_add(cnt[:, 0:1], cnt[:, 0:1],
                                                     cnt[:, 1:2]),
                         reads=[("bcnt",), ("bcnt1",)], writes=[("bcnt",)])
            p.op("V", lambda e: e.tensor_scalar(cnt[:, 1:2], cnt[:, 0:1],
                                                float(C), None, ALU.is_ge),
                 reads=[("bcnt",)], writes=[("bsel",)])
            p.op("V", lambda e: e.tensor_copy(selb[:, :], cnt[:, 1:2]),
                 reads=[("bsel",)], writes=[("bselb",)])
            p.op("T", lambda e: e.matmul(env.ps[6][0:1, 0:1],
                                         env.ones_col_bf[:, :],
                                         selb[:, :], start=True, stop=True),
                 reads=[("bselb",), ("ones_col_bf",)], writes=[("ps", 6)])
            p.op("V", lambda e: e.tensor_copy(bis[0:1, 3:4], env.ps[6][0:1, 0:1]),
                 reads=[("ps", 6)], writes=[("bs",)])
            p.op("V", lambda e: e.tensor_mul(bis[0:1, 4:5], bis[0:1, 3:4],
                                             bis[0:1, 2:3]),
                 reads=[("bs",), ("bd",)], writes=[("bsd",)])
            p.op("V", lambda e: e.tensor_add(bis[0:1, 0:1], bis[0:1, 0:1],
                                             bis[0:1, 4:5]),
                 reads=[("blo",), ("bsd",)], writes=[("blo",)])
            p.op("V", lambda e: e.tensor_add(bis[0:1, 1:2], bis[0:1, 0:1],
                                             bis[0:1, 2:3]),
                 reads=[("blo",), ("bd",)], writes=[("bhi",)])
        p.op("T", lambda e: e.matmul(env.ps[6][:, 0:1], env.ones_row_f[:, :],
                                     bis[0:1, 0:1], start=True, stop=True),
             reads=[("blo",), ("ones_row_f",)], writes=[("ps", 6)])
        p.op("V", lambda e: e.tensor_copy(bcol[:, 1:2], env.ps[6][:, 0:1]),
             reads=[("ps", 6)], writes=[("blob",)])
        for kc in range(ntc):
            bk = 6 + kc % 2
            p.op("T", lambda e, kc=kc, bk=bk: e.transpose(
                env.ps[bk][:, 0:128], gbc[:, kc * 128:(kc + 1) * 128],
                env.ident_f[:, :]),
                reads=gbr + [("ident_f",)], writes=[("ps", bk)])
            p.op("V", lambda e, kc=kc, bk=bk: e.tensor_copy(
                gtm[:, kc:kc + 1], env.ps[bk][:, 0:1]),
                reads=[("ps", bk)], writes=[("gtm", kc)])
        p.op("V", lambda e: e.tensor_scalar(gtm[:, 64:64 + ntc], gtm[:, 0:ntc],
                                            bcol[:, 1:2], None, ALU.is_ge),
             reads=[("gtm", kc) for kc in range(ntc)] + [("blob",)],
             writes=[("mtm",)])
        p.op("V", lambda e: e.tensor_scalar(gtm[:, 32:32 + ntc],
                                            gtm[:, 64:64 + ntc],
                                            1.0, 50.0, ALU.subtract, ALU.mult),
             reads=[("mtm",)], writes=[("mneg",)])

        if dbg:
            dbg_h = nc.dram_tensor("dbg", [128, 2 * T + 64], F32,
                                   kind="ExternalOutput").ap()
            p.dma("DY", dbg_h[:, 0:T], gbc[:, :], reads=gbr, writes=[("dbgH",)])
            p.dma("DY", dbg_h[0:1, T:T + 8], bis[0:1, 0:8],
                  reads=[("blo",), ("bhi",)], writes=[("dbgH2",)])
            p.dma("DY", dbg_h[:, T + 16:T + 16 + ntc], gtm[:, 64:64 + ntc],
                  reads=[("mtm",)], writes=[("dbgH3",)])
        p.op("G", lambda e: e.memset(vhb[:, :], 1.0),
             writes=[("vh", tc) for tc in range(ntc)])
        for hp in range(nch):
            wp = wsl[hp % 2]
            p.dma("DG", wo_sb[hp % 2][:, :].rearrange("p (c n) -> p c n", c=1),
                  wo_h.rearrange("(c p) n -> p c n", p=128)[:, hp:hp + 1, :],
                  writes=[("wo", hp % 2)], tag=f"wo{hp}", lane=0)
            for j, wsrc in enumerate((wq_h, wk_h, wv_h)):
                p.dma("DG", wp[j][:, :].rearrange("p (c n) -> p c n", n=128),
                      wsrc.rearrange("(c p) n -> p c n", p=128)
                      [:, :, hp * 128:(hp + 1) * 128],
                      writes=[("wsl", hp % 2, j)], tag=f"w{hp}", lane=1)
            for tt in range(ntt):
                sl = slice(tt * 512, (tt + 1) * 512)
                p.dma("DY" if tt % 2 == 0 else "DA",
                      xstr[tt % 2][:, :]
                      .rearrange("p (c t) -> p c t", t=512),
                      xh1_h.rearrange("(c p) t -> p c t", p=128)[:, :, sl],
                      reads=[("xh1H", dc, tt) for dc in range(nch)],
                      writes=[("st", tt % 2)])
                for j, dstb, rd in ((0, qhb, ("qh",)), (1, khb, ("kh",))):
                    bk = 4 + j
                    p.op("T", lambda e, j=j, bk=bk, tt=tt, wp=wp: [
                        e.matmul(env.ps[bk][:, :],
                                 wp[j][:, dc * 128:(dc + 1) * 128],
                                 xstr[tt % 2][:, dc * 512:(dc + 1) * 512],
                                 start=(dc == 0), stop=(dc == nch - 1))
                        for dc in range(nch)][-1],
                        reads=[("wsl", hp % 2, j), ("st", tt % 2)],
                        writes=[("ps", bk)])
                    p.op("V", lambda e, bk=bk, dstb=dstb, sl=sl: e.tensor_copy(
                        dstb[:, sl], env.ps[bk][:, :]),
                        reads=[("ps", bk)], writes=[rd])
                for tq in range(4):
                    tc = tt * 4 + tq
                    bk = 6 + tc % 2
                    p.op("T", lambda e, tc=tc, tq=tq, bk=bk, tt=tt, wp=wp: [
                        e.matmul(env.ps[bk][:, 0:128],
                                 xstr[tt % 2][:, dc * 512 + tq * 128:
                                              dc * 512 + (tq + 1) * 128],
                                 wp[2][:, dc * 128:(dc + 1) * 128],
                                 start=(dc == 0), stop=(dc == nch - 1))
                        for dc in range(nch)][-1],
                        reads=[("wsl", hp % 2, 2), ("st", tt % 2)],
                        writes=[("ps", bk)])
                    p.op("V", lambda e, tc=tc, bk=bk: [
                        e.tensor_copy(vhb[:, tc * 130:tc * 130 + 64],
                                      env.ps[bk][:, 0:64]),
                        e.tensor_copy(vhb[:, tc * 130 + 65:tc * 130 + 129],
                                      env.ps[bk][:, 64:128])][-1],
                        reads=[("ps", bk)], writes=[("vh", tc)])
            for hh in range(2):
                base = hh * 64
                tp = (base, 0)
                for src_b, so, rs in ((khb, 0, ("kh",)), (qhb, 1, ("qh",))):
                    for tt in range(ntt):
                        sq = sqp[tt % 2]
                        sl = slice(tt * 512, (tt + 1) * 512)
                        p.op("S", lambda e, src_b=src_b, sl=sl, base=base,
                             sq=sq: e.activation(
                                 sq[0:64, :], src_b[base:base + 64, sl],
                                 AF.Square),
                             reads=[rs], writes=[("atsq", tt % 2)])
                        p.op("T", lambda e, sq=sq: e.matmul(
                            env.ps[6][0:1, 0:512], env.ones_col_bf[0:64, :],
                            sq[0:64, :], start=True, stop=True,
                            tile_position=(0, 0)),
                            reads=[("atsq", tt % 2), ("ones_col_bf",)],
                            writes=[("ps", 6)])
                        if tt == 0:
                            p.op("V", lambda e, so=so: e.reduce_max(
                                nrm[0:1, NS + so:NS + so + 1],
                                env.ps[6][0:1, 0:512], AX.X),
                                reads=[("ps", 6)], writes=[("nmx", so)])
                        else:
                            p.op("V", lambda e: e.reduce_max(
                                nrm[0:1, NS + 4:NS + 5],
                                env.ps[6][0:1, 0:512], AX.X),
                                reads=[("ps", 6)], writes=[("nmt",)])
                            p.op("V", lambda e, so=so: e.tensor_max(
                                nrm[0:1, NS + so:NS + so + 1],
                                nrm[0:1, NS + so:NS + so + 1],
                                nrm[0:1, NS + 4:NS + 5]),
                                reads=[("nmt",), ("nmx", so)],
                                writes=[("nmx", so)])
                for so in (0, 1):
                    p.op("S", lambda e, so=so: e.activation(
                        nrm[0:1, NS + 2 + so:NS + 3 + so],
                        nrm[0:1, NS + so:NS + so + 1], AF.Sqrt),
                        reads=[("nmx", so)], writes=[("nsq", so)])
                p.op("V", lambda e: e.tensor_mul(
                    nrm[0:1, NS + 5:NS + 6], nrm[0:1, NS + 2:NS + 3],
                    nrm[0:1, NS + 3:NS + 4]),
                    reads=[("nsq", 0), ("nsq", 1)], writes=[("ncc",)])
                p.op("V", lambda e: e.tensor_scalar(
                    nrm[0:1, NS + 6:NS + 7], nrm[0:1, NS + 5:NS + 6],
                    -SCL, None, ALU.mult),
                    reads=[("ncc",)], writes=[("nbc",)])
                p.op("T", lambda e: e.matmul(env.ps[6][:, 0:1],
                                             env.ones_row_f[:, :],
                                             nrm[0:1, NS + 6:NS + 7],
                                             start=True, stop=True),
                     reads=[("nbc",), ("ones_row_f",)], writes=[("ps", 6)])
                p.op("V", lambda e, hh=hh: e.tensor_copy(negc[:, hh:hh + 1],
                                                         env.ps[6][:, 0:1]),
                     reads=[("ps", 6)], writes=[("negc", hh)])
                p.op("V", lambda e, hh=hh: e.tensor_scalar(
                    bch[:, hh * 32:hh * 32 + ntc], gtm[:, 32:32 + ntc],
                    negc[:, hh:hh + 1], None, ALU.add),
                    reads=[("mneg",), ("negc", hh)], writes=[("bch", hh)])
                def norm_f1(qt, abk, qsl, base=base):
                    p.op("V", lambda e, abk=abk: e.reciprocal(
                        rscr[0:1, :], env.ps[abk][64:65, :]),
                        reads=[("ps", abk)], writes=[("nrec",)])
                    p.op("V", lambda e: e.tensor_copy(rrow[0:1, :],
                                                      rscr[0:1, :]),
                         reads=[("nrec",)], writes=[("nrecb",)])
                    p.op("T", lambda e, qt=qt: e.matmul(
                        env.ps[qt % 2][0:64, :], env.ones_row_bf[:, 0:64],
                        rrow[0:1, :], start=True, stop=True),
                        reads=[("nrecb",), ("ones_row_bf",)],
                        writes=[("ps", qt % 2)])
                    p.op("V", lambda e, qt=qt: e.tensor_copy(
                        rbt[qt % 2][0:64, :], env.ps[qt % 2][0:64, :]),
                        reads=[("ps", qt % 2)], writes=[("rbt", qt % 2)])
                    p.op("V", lambda e, qt=qt, qsl=qsl, abk=abk, base=base:
                         e.tensor_mul(oTp[base:base + 64, qsl],
                                      env.ps[abk][0:64, :],
                                      rbt[qt % 2][0:64, :]),
                         reads=[("ps", abk), ("rbt", qt % 2)],
                         writes=[("oTp",)])
                pend = None
                for qt in range(ntt):
                    qsl = slice(qt * 512, (qt + 1) * 512)
                    abk = 6 + qt % 2
                    voff = 0 if base == 0 else 65
                    LA = 2
                    for step in range(ntc + LA):
                        if step == 4 and pend is not None:
                            norm_f1(*pend)
                            pend = None
                        if step < ntc:
                            kc = step
                            sbk = 2 + kc % 4
                            pt = ptb[kc % 4]
                            p.op("T", lambda e, kc=kc, qsl=qsl, base=base,
                                 sbk=sbk, tp=tp: e.matmul(
                                     env.ps[sbk][:, :],
                                     khb[base:base + 64,
                                         kc * 128:(kc + 1) * 128],
                                     qhb[base:base + 64, qsl],
                                     start=True, stop=True, tile_position=tp),
                                 reads=[("kh",), ("qh",)],
                                 writes=[("ps", sbk)])
                            p.op("S", lambda e, pt=pt, sbk=sbk, kc=kc, hh=hh:
                                 e.activation(
                                     pt[:, :], env.ps[sbk][:, :], AF.Exp,
                                     bias=bch[:, hh * 32 + kc:
                                              hh * 32 + kc + 1],
                                     scale=sclc[:, 0:1]),
                                 reads=[("ps", sbk), ("bch", hh), ("sclc",)],
                                 writes=[("pt", kc % 4)])
                        if step >= LA:
                            kc = step - LA
                            pt = ptb[kc % 4]
                            p.op("T", lambda e, pt=pt, kc=kc, voff=voff,
                                 abk=abk: e.matmul(
                                     env.ps[abk][0:65, :],
                                     vhb[:, kc * 130 + voff:
                                         kc * 130 + voff + 65],
                                     pt[:, :],
                                     start=(kc == 0), stop=(kc == ntc - 1)),
                                 reads=[("pt", kc % 4), ("vh", kc)],
                                 writes=[("ps", abk)])
                    pend = (qt, abk, qsl)
                norm_f1(*pend)
            if dbg and hp == 0:
                p.dma("DG", dbg_h[:, T + 64:2 * T + 64],
                      oTp[:, 0:T], reads=[("oTp",)], writes=[("dbgO",)])
            for dc_o in range(nch):
                for tt in range(ntt):
                    bk = 4 + (dc_o * ntt + tt) % 2
                    sl = slice(tt * 512, (tt + 1) * 512)
                    sp = (dc_o * ntt + tt) % 2
                    p.op("T", lambda e, dc_o=dc_o, sl=sl, bk=bk, hp=hp:
                         e.matmul(
                             env.ps[bk][:, :],
                             wo_sb[hp % 2][:, dc_o * 128:(dc_o + 1) * 128],
                             oTp[:, sl], start=True, stop=True),
                         reads=[("wo", hp % 2), ("oTp",)],
                         writes=[("ps", bk)])
                    p.op("V", lambda e, bk=bk, sp=sp: e.tensor_copy(
                        fx[sp][:, :], env.ps[bk][:, :]),
                        reads=[("ps", bk)], writes=[("fx", sp)])
                    kw = {} if hp == 0 else {"accum_op": ALU.add}
                    p.dma("DG", oacc_h.rearrange("(c p) t -> p c t", p=128)
                          [:, dc_o:dc_o + 1, sl],
                          fx[sp][:, :].rearrange("p (c t) -> p c t", c=1),
                          reads=[("fx", sp)], writes=[("oaH", dc_o, tt)],
                          lane=2 + sp, **kw)

        for tt in range(ntt):
            sl = slice(tt * 512, (tt + 1) * 512)
            gp = tt % 2
            p.op("V", lambda e, sl=sl, gp=gp: e.tensor_scalar(
                gmb[gp][:, :], gbc[:, sl], bcol[:, 1:2], None, ALU.is_ge),
                reads=gbr + [("blob",)], writes=[("gmb", gp)])
            p.op("V", lambda e, sl=sl, gp=gp: e.tensor_mul(
                gmb[gp][:, :], gmb[gp][:, :], gbc[:, sl]),
                reads=[("gmb", gp)] + gbr, writes=[("gmb", gp)])
            for dc_o in range(nch):
                sp = dc_o % 2
                p.dma("DY", fr[sp][:, :].rearrange("p (c t) -> p c t", c=1),
                      oacc_h.rearrange("(c p) t -> p c t", p=128)
                      [:, dc_o:dc_o + 1, sl],
                      reads=[("oaH", dc_o, tt)], writes=[("fr", sp)])
                p.op("V", lambda e, sp=sp, gp=gp: e.tensor_mul(
                    fr[sp][:, :], fr[sp][:, :], gmb[gp][:, :]),
                    reads=[("fr", sp), ("gmb", gp)], writes=[("fr", sp)])
                p.dma("DA", fx[sp][:, :].rearrange("p (c t) -> p c t", c=1),
                      x_h.rearrange("(c p) t -> p c t", p=128)
                      [:, dc_o:dc_o + 1, sl],
                      reads=[], writes=[("fx", sp)])
                p.op("V", lambda e, sp=sp: e.tensor_add(
                    fr[sp][:, :], fr[sp][:, :], fx[sp][:, :]),
                    reads=[("fr", sp), ("fx", sp)], writes=[("fr", sp)])
                p.dma("DY", y_h.rearrange("(c p) t -> p c t", p=128)
                      [:, dc_o:dc_o + 1, sl],
                      fr[sp][:, :].rearrange("p (c t) -> p c t", c=1),
                      reads=[("fr", sp)], writes=[("yH", dc_o, tt)])
        if fused is None:
            p.wait_all("Y")
            block = ctx.enter_context(nc.Block())
            p.emit(block)
    return nc



def build_b1_program(T=4096, fused=None, pfx=""):
    nch = 8
    ntt = T // 512
    NH = 16
    nc = fused["nc"] if fused else bass.Bass("TRN2", target_bir_lowering=False,
                                             debug=False)
    x_h = fused["x"] if fused else dram(nc, pfx + "x", [D, T], F32, "ExternalInput")
    qw_h = dram(nc, pfx + "qw", [D, D], F32, "ExternalInput")
    kw_h = dram(nc, pfx + "kw", [D, D], F32, "ExternalInput")
    vw_h = dram(nc, pfx + "vw", [D, D], F32, "ExternalInput")
    ow_h = dram(nc, pfx + "ow", [D, D], F32, "ExternalInput")
    cs1_h = dram(nc, pfx + "cs1", [D, 256], F32, "ExternalInput")
    cb1_h = dram(nc, pfx + "csb1_2", [128, 2], F32, "ExternalInput")
    cs2_h = dram(nc, pfx + "cs2", [256, 16], F32, "ExternalInput")
    cb2_h = dram(nc, pfx + "csb2c", [16, 1], F32, "ExternalInput")
    dpr_h = dram(nc, pfx + "dpr", [1, D], F32, "ExternalInput")
    y_h = fused["y"] if fused else dram(nc, pfx + "y", [D, T], F32, "ExternalOutput")
    xh_h = dram(nc, pfx + "b1xh", [D, T], BF16, "Internal")

    with (ExitStack() if fused is None else _KeepOpen(fused["ctx"])) as ctx:
        if fused is None:
            p = Prog(nc, ctx)
            env = Env(nc, ctx, p)
            pool = Pool(env, POOL_COLS)
            p.ns = "s"
        else:
            p, env, pool = fused["p"], fused["env"], fused["pool"]
            pool.reset()
            p.ns = pfx
        p.barrier()
        lnsrc = [pool.carve(nch * 512, BF16, parts=128) for i in range(2)]
        xstr = lnsrc
        ostg = [pool.carve(512, BF16, parts=128) for i in range(2)]
        wq_sb = pool.carve(nch * D, BF16, parts=128)
        wk_sb = pool.carve(nch * D, BF16, parts=128)
        wv_sb = pool.carve(nch * D, BF16, parts=128)
        wo_sb2 = pool.carve(nch * D, BF16, parts=128)
        cs1_sb = pool.carve(nch * 256, BF16, parts=128)
        cs2_sb = pool.carve(2 * 16, BF16, parts=128)
        cb1_sb = pool.carve(2, F32, parts=128)
        cb2_sb = pool.carve(1, F32, parts=16)
        lcf = pool.carve(T, BF16, parts=16)
        lcB = pool.carve(D, BF16, parts=16)
        sigd = pool.carve(D, F32, parts=128)
        sgrow = pool.carve(D, F32, parts=1)
        qf = pool.carve(nch * 512, BF16, parts=128)
        kf = pool.carve(nch * 512, BF16, parts=128)
        ktm = pool.carve(4 * D, BF16, parts=128)
        vtm = pool.carve(4 * D, BF16, parts=128)
        ofm0 = pool.carve(nch * 512, BF16, parts=128)
        ofm = [ofm0, ofm0]
        lamt = pool.carve(D, F32, parts=128)
        llb = pool.carve(D, F32, parts=128)
        efb = [pool.carve(D, F32, parts=128) for i in range(2)]
        qhat = pool.carve(D, BF16, parts=128)
        qabs = pool.carve(D, F32, parts=128)
        khat = [pool.carve(D, BF16, parts=128) for i in range(4)]
        kst = pool.carve(D, BF16, parts=128)
        eLt = pool.carve(8, F32, parts=128)
        state = pool.carve(8 * 64, F32, parts=128)
        ptb = [pool.carve(128, BF16, parts=128) for i in range(2)]
        fx = [pool.carve(512, F32, parts=128) for i in range(2)]
        lnb = pool_ln_bufs(pool)

        tri = pool.carve(128, F32, parts=128)
        trisub = pool.carve(128, F32, parts=128)
        triu = pool.carve(128, F32, parts=128)
        ci = [pool.carve(128, F32, parts=128) for i in range(4)]
        p.op("G", lambda e: e.memset(tri[:, :], 1.0), writes=[("tri",)])
        p.op("G", lambda e: e.affine_select(
            out=tri[:, :], in_=tri[:, :], compare_op=ALU.is_ge, fill=0.0,
            base=0, pattern=[[1, 128]], channel_multiplier=-1),
            reads=[("tri",)], writes=[("tri",)])
        p.op("G", lambda e: e.memset(trisub[:, :], 1.0), writes=[("tsb",)])
        p.op("G", lambda e: e.affine_select(
            out=trisub[:, :], in_=trisub[:, :], compare_op=ALU.is_ge,
            fill=0.0, base=0, pattern=[[1, 128]], channel_multiplier=-1),
            reads=[("tsb",)], writes=[("tsb",)])
        p.op("G", lambda e: e.affine_select(
            out=trisub[:, :].rearrange("p (a b) -> p a b", b=32),
            in_=trisub[:, :].rearrange("p (a b) -> p a b", b=32),
            compare_op=ALU.is_ge, fill=0.0,
            base=0, pattern=[[-32, 4], [0, 32]], channel_multiplier=1),
            reads=[("tsb",)], writes=[("tsb",)])
        p.op("G", lambda e: e.memset(triu[:, :], 1.0), writes=[("triu",)])
        p.op("G", lambda e: e.affine_select(
            out=triu[:, :], in_=triu[:, :], compare_op=ALU.is_gt, fill=0.0,
            base=0, pattern=[[-1, 128]], channel_multiplier=1),
            reads=[("triu",)], writes=[("triu",)])
        for i in range(4):
            p.op("G", lambda e, i=i: e.memset(ci[i][:, :], 1.0),
                 writes=[("ci", i)])
            p.op("G", lambda e, i=i: e.affine_select(
                out=ci[i][:, :], in_=ci[i][:, :], compare_op=ALU.is_ge,
                fill=0.0, base=32 * i - 1, pattern=[[0, 128]],
                channel_multiplier=-1),
                reads=[("ci", i)], writes=[("ci", i)])
            p.op("V", lambda e, i=i: e.tensor_sub(ci[i][:, :], ci[i][:, :],
                                                  tri[:, :]),
                 reads=[("ci", i), ("tri",)], writes=[("ci", i)])
        p.op("G", lambda e: e.memset(state[:, :], 0.0),
             writes=[("st8", h) for h in range(NH)])
        p.op("G", lambda e: e.memset(lcB[:, :], 0.0), writes=[("lcB",)])
        p.op("G", lambda e: e.affine_select(
            out=lcB[:, :].rearrange("p (a b) -> p a b", b=64),
            in_=lcB[:, :].rearrange("p (a b) -> p a b", b=64),
            compare_op=ALU.not_equal, fill=1.0, base=0,
            pattern=[[-1, 16], [0, 64]], channel_multiplier=1),
            reads=[("lcB",)], writes=[("lcB",)])

        for wsb, wh, rr in ((wq_sb, qw_h, "wq"), (wk_sb, kw_h, "wk"),
                            (wv_sb, vw_h, "wv"), (wo_sb2, ow_h, "wo")):
            p.dma("DG", wsb[:, :].rearrange("p (c n) -> p c n", n=D),
                  wh.rearrange("(c p) n -> p c n", p=128),
                  writes=[(rr,)])
        p.dma("DG", cs1_sb[:, :].rearrange("p (c n) -> p c n", n=256),
              cs1_h.rearrange("(c p) n -> p c n", p=128), writes=[("cs1",)])
        p.dma("DG", cs2_sb[:, :].rearrange("p (c n) -> p c n", n=16),
              cs2_h.rearrange("(c p) n -> p c n", p=128), writes=[("cs2",)])
        p.dma("DY", cb1_sb[:, :], cb1_h, writes=[("cb1",)])
        p.dma("DY", cb2_sb[:, :], cb2_h, writes=[("cb2",)])
        p.dma("DY", sgrow[:, :], dpr_h, writes=[("dprr",)])
        p.op("S", lambda e: e.activation(sgrow[0:1, :], sgrow[0:1, :],
                                         AF.Sigmoid),
             reads=[("dprr",)], writes=[("dprr",)])
        for i in range(2):
            p.op("T", lambda e, i=i: e.matmul(
                env.ps[6][:, :], env.ones_row_f[:, :],
                sgrow[0:1, i * 512:(i + 1) * 512], start=True, stop=True),
                reads=[("dprr",), ("ones_row_f",)], writes=[("ps", 6)])
            p.op("V", lambda e, i=i: e.tensor_copy(
                sigd[:, i * 512:(i + 1) * 512], env.ps[6][:, :]),
                reads=[("ps", 6)], writes=[("sigd",)])

        def pre0(tt):
            p.dma("DG", lnsrc[tt % 2][:, :].rearrange("p (c t) -> p c t", t=512),
                  x_h.rearrange("(c p) t -> p c t", p=128)
                  [:, :, tt * 512:(tt + 1) * 512],
                  writes=[("st", tt % 2)])

        def post0(dc, tt):
            p.dma("DY", xh_h.rearrange("(c p) t -> p c t", p=128)
                  [:, dc:dc + 1, tt * 512:(tt + 1) * 512],
                  ostg[dc % 2][:, :].rearrange("p (c t) -> p c t", c=1),
                  reads=[("ostg", dc % 2)], writes=[("xhH", dc, tt)])
        emit_ln(env, T,
                src=lambda dc, tt: lnsrc[tt % 2][:, dc * 512:(dc + 1) * 512],
                dst=lambda dc, tt: ostg[dc % 2][:, :],
                rsrc=lambda dc, tt: ("st", tt % 2),
                rdst=lambda dc, tt: ("ostg", dc % 2),
                bufs=lnb, preload=pre0, post=post0)

        tanbuf = pool.carve(2 * 512, BF16, parts=128)
        for tt in range(ntt):
            sl = slice(tt * 512, (tt + 1) * 512)
            p.dma("DY", xstr[tt % 2][:, :].rearrange("p (c t) -> p c t", t=512),
                  xh_h.rearrange("(c p) t -> p c t", p=128)[:, :, sl],
                  reads=[("xhH", dc, tt) for dc in range(nch)],
                  writes=[("st", tt % 2)])
            for rc in range(2):
                bk = 4 + rc
                p.op("T", lambda e, rc=rc, bk=bk, tt=tt: [
                    e.matmul(env.ps[bk][:, :],
                             cs1_sb[:, dc * 256 + rc * 128:
                                    dc * 256 + (rc + 1) * 128],
                             xstr[tt % 2][:, dc * 512:(dc + 1) * 512],
                             start=(dc == 0), stop=(dc == nch - 1))
                    for dc in range(nch)][-1],
                    reads=[("cs1",), ("st", tt % 2)], writes=[("ps", bk)])
                p.op("S", lambda e, rc=rc, bk=bk: e.activation(
                    tanbuf[:, rc * 512:(rc + 1) * 512], env.ps[bk][:, :],
                    AF.Tanh, bias=cb1_sb[:, rc:rc + 1]),
                    reads=[("ps", bk), ("cb1",)], writes=[("tan", rc)])
            p.op("T", lambda e, tt=tt: [
                e.matmul(env.ps[6][0:16, :],
                         cs2_sb[:, rc * 16:(rc + 1) * 16],
                         tanbuf[:, rc * 512:(rc + 1) * 512],
                         start=(rc == 0), stop=(rc == 1))
                for rc in range(2)][-1],
                reads=[("cs2",), ("tan", 0), ("tan", 1)], writes=[("ps", 6)])
            p.op("S", lambda e, sl=sl, tt=tt: e.activation(
                lcf[0:16, sl], env.ps[6][0:16, :], AF.Sigmoid,
                bias=cb2_sb[0:16, 0:1]),
                reads=[("ps", 6), ("cb2",)], writes=[("lcf", tt)])

        for tt in range(ntt):
            sl = slice(tt * 512, (tt + 1) * 512)
            op_ = ofm[tt % 2]
            p.dma("DY", xstr[tt % 2][:, :].rearrange("p (c t) -> p c t", t=512),
                  xh_h.rearrange("(c p) t -> p c t", p=128)[:, :, sl],
                  reads=[("xhH", dc, tt) for dc in range(nch)],
                  writes=[("st", tt % 2)])
            for dst, wsb, rw, rd in ((qf, wq_sb, ("wq",), ("qf",)),
                                     (kf, wk_sb, ("wk",), ("kf",))):
                for nc_o in range(nch):
                    bk = 4 + nc_o % 2
                    p.op("T", lambda e, nc_o=nc_o, bk=bk, tt=tt, wsb=wsb: [
                        e.matmul(env.ps[bk][:, :],
                                 wsb[:, dc * D + nc_o * 128:
                                     dc * D + (nc_o + 1) * 128],
                                 xstr[tt % 2][:, dc * 512:(dc + 1) * 512],
                                 start=(dc == 0), stop=(dc == nch - 1))
                        for dc in range(nch)][-1],
                        reads=[rw, ("st", tt % 2)], writes=[("ps", bk)])
                    p.op("V", lambda e, nc_o=nc_o, bk=bk, dst=dst:
                         e.tensor_copy(dst[:, nc_o * 512:(nc_o + 1) * 512],
                                       env.ps[bk][:, :]),
                         reads=[("ps", bk)], writes=[rd])
            for dst, wsb, rw, rd in ((ktm, wk_sb, ("wk",), ("ktm",)),
                                     (vtm, wv_sb, ("wv",), ("vtm",))):
                for tq in range(4):
                    for nt in range(2):
                        bk = 4 + (tq * 2 + nt) % 2
                        p.op("T", lambda e, tq=tq, nt=nt, bk=bk, tt=tt,
                             wsb=wsb: [
                            e.matmul(env.ps[bk][:, :],
                                     xstr[tt % 2][:, dc * 512 + tq * 128:
                                                  dc * 512 + (tq + 1) * 128],
                                     wsb[:, dc * D + nt * 512:
                                         dc * D + (nt + 1) * 512],
                                     start=(dc == 0), stop=(dc == nch - 1))
                            for dc in range(nch)][-1],
                            reads=[rw, ("st", tt % 2)], writes=[("ps", bk)])
                        p.op("V", lambda e, tq=tq, nt=nt, bk=bk, dst=dst:
                             e.tensor_copy(
                                 dst[:, tq * D + nt * 512:
                                     tq * D + (nt + 1) * 512],
                                 env.ps[bk][:, :]),
                             reads=[("ps", bk)], writes=[rd])
            for gq in range(4):
                g0 = gq * 128
                csl = slice(tt * 512 + g0, tt * 512 + g0 + 128)
                qf3 = qf[:, :].rearrange("p (c t) -> p c t", t=512)[:, :, g0:g0 + 128]
                kf3 = kf[:, :].rearrange("p (c t) -> p c t", t=512)[:, :, g0:g0 + 128]
                ktm2 = ktm[:, gq * D:(gq + 1) * D]
                vtm2 = vtm[:, gq * D:(gq + 1) * D]
                for half in range(2):
                    bk = 4 + half
                    p.op("T", lambda e, half=half, bk=bk, csl=csl, tt=tt:
                         e.matmul(env.ps[bk][:, :], lcf[0:16, csl],
                                  lcB[0:16, half * 512:(half + 1) * 512],
                                  start=True, stop=True),
                         reads=[("lcf", tt), ("lcB",)], writes=[("ps", bk)])
                    p.op("V", lambda e, half=half, bk=bk: e.tensor_scalar(
                        lamt[:, half * 512:(half + 1) * 512],
                        env.ps[bk][:, :], 0.2, 1.0, ALU.mult, ALU.add),
                        reads=[("ps", bk)], writes=[("lam", half)])
                    p.op("V", lambda e, half=half: e.tensor_mul(
                        lamt[:, half * 512:(half + 1) * 512],
                        lamt[:, half * 512:(half + 1) * 512],
                        sigd[:, half * 512:(half + 1) * 512]),
                        reads=[("lam", half), ("sigd",)],
                        writes=[("lam", half)])
                    p.op("V", lambda e, half=half: e.tensor_scalar(
                        lamt[:, half * 512:(half + 1) * 512],
                        lamt[:, half * 512:(half + 1) * 512],
                        0.9995, None, ALU.min),
                        reads=[("lam", half)], writes=[("lam", half)])
                    p.op("S", lambda e, half=half: e.activation(
                        llb[:, half * 512:(half + 1) * 512],
                        lamt[:, half * 512:(half + 1) * 512], AF.Ln),
                        reads=[("lam", half)], writes=[("llb", half)])
                rll = [("llb", 0), ("llb", 1)]
                def fm_exparrays(rhs_const, rconst, dst, rdst, mulsrc3, rmul,
                                 dstname):
                    for grp in range(2):
                        bk = 4 + grp
                        p.op("T", lambda e, grp=grp, bk=bk,
                             rhs_const=rhs_const: [
                            e.matmul(env.ps[bk][:, s4 * 128:(s4 + 1) * 128],
                                     llb[:, (grp * 4 + s4) * 128:
                                         (grp * 4 + s4 + 1) * 128],
                                     rhs_const[:, :],
                                     start=True, stop=True)
                            for s4 in range(4)][-1],
                            reads=rll + [rconst], writes=[("ps", bk)])
                        p.op("S", lambda e, grp=grp, bk=bk, dst=dst:
                             e.activation(
                                 efb[grp][:, 0:512], env.ps[bk][:, :], AF.Exp),
                             reads=[("ps", bk)], writes=[("efb", grp)])
                        p.op("V", lambda e, grp=grp, dst=dst,
                             mulsrc3=mulsrc3: e.tensor_mul(
                                 dst[:, grp * 512:(grp + 1) * 512]
                                 .rearrange("p (c t) -> p c t", t=128),
                                 mulsrc3[:, grp * 4:(grp + 1) * 4, :],
                                 efb[grp][:, 0:512]
                                 .rearrange("p (c t) -> p c t", t=128)),
                             reads=[("efb", grp), rmul], writes=[rdst])
                fm_exparrays(trisub, ("tsb",), qhat, ("qhat",), qf3, ("qf",),
                             "qhat")
                fm_exparrays(tri, ("tri",), qabs, ("qabs",), qf3, ("qf",),
                             "qabs")
                for i in range(4):
                    fm_exparrays(ci[i], ("ci", i), khat[i], ("khat", i),
                                 kf3, ("kf",), f"khat{i}")
                for half in range(2):
                    bk = 4 + half
                    p.op("T", lambda e, half=half, bk=bk: e.matmul(
                        env.ps[bk][:, :], triu[:, :],
                        llb[:, half * 512:(half + 1) * 512],
                        start=True, stop=True),
                        reads=rll + [("triu",)], writes=[("ps", bk)])
                    p.op("S", lambda e, half=half, bk=bk: e.activation(
                        efb[half][:, 0:512], env.ps[bk][:, :], AF.Exp),
                        reads=[("ps", bk)], writes=[("efb", half)])
                    p.op("V", lambda e, half=half, ktm2=ktm2: e.tensor_mul(
                        kst[:, half * 512:(half + 1) * 512],
                        ktm2[:, half * 512:(half + 1) * 512],
                        efb[half][:, 0:512]),
                        reads=[("efb", half), ("ktm",)], writes=[("kst",)])
                p.op("T", lambda e: [
                    e.matmul(env.ps[6][:, s8:s8 + 1],
                             llb[:, s8 * 128:(s8 + 1) * 128],
                             env.ones_col_f[:, 0:1], start=True, stop=True)
                    for s8 in range(8)][-1],
                    reads=rll + [("ones_col_f",)], writes=[("ps", 6)])
                p.op("S", lambda e: e.activation(eLt[:, 0:8],
                                                 env.ps[6][:, 0:8], AF.Exp),
                     reads=[("ps", 6)], writes=[("eLt",)])
                for h in range(NH):
                    base = (h % 2) * 64
                    hs = h // 2
                    tp = (base, 0)
                    sbk = 4 + h % 2
                    pt = ptb[h % 2]
                    p.op("T", lambda e, h=h, base=base, hs=hs, sbk=sbk,
                         tp=tp: [
                        e.matmul(env.ps[sbk][0:32 * (i + 1),
                                            i * 32:(i + 1) * 32],
                                 khat[i][base:base + 64,
                                         hs * 128:hs * 128 + 32 * (i + 1)],
                                 qhat[base:base + 64,
                                      hs * 128 + 32 * i:
                                      hs * 128 + 32 * (i + 1)],
                                 start=True, stop=True, tile_position=tp)
                        for i in range(4)][-1],
                        reads=[("khat", i) for i in range(4)] + [("qhat",)],
                        writes=[("ps", sbk)])
                    p.op("V", lambda e, pt=pt, sbk=sbk: e.tensor_mul(
                        pt[:, :], env.ps[sbk][:, 0:128], tri[:, :]),
                        reads=[("ps", sbk), ("tri",)], writes=[("ptb", h % 2)])
                    abk = 6 + h % 2
                    p.op("T", lambda e, h=h, pt=pt, abk=abk, vtm2=vtm2:
                         e.matmul(env.ps[abk][0:64, 0:128],
                                  vtm2[:, h * 64:(h + 1) * 64], pt[:, :],
                                  start=True, stop=False),
                         reads=[("ptb", h % 2), ("vtm",)],
                         writes=[("ps", abk)])
                    p.op("T", lambda e, h=h, base=base, hs=hs, abk=abk, tp=tp:
                         e.matmul(env.ps[abk][0:64, 0:128],
                                  state[base:base + 64,
                                        hs * 64:(hs + 1) * 64],
                                  qabs[base:base + 64,
                                       hs * 128:(hs + 1) * 128],
                                  start=False, stop=True, tile_position=tp),
                         reads=[("st8", h), ("qabs",)], writes=[("ps", abk)])
                    p.op("V", lambda e, h=h, base=base, hs=hs, abk=abk,
                         op_=op_, gq=gq: e.tensor_copy(
                             op_[base:base + 64,
                                 hs * 512 + gq * 128:hs * 512 + (gq + 1) * 128],
                             env.ps[abk][0:64, 0:128]),
                         reads=[("ps", abk)], writes=[("ofm", 0, hs)])
                    ubk = 2 + h % 2
                    p.op("T", lambda e, h=h, ubk=ubk, vtm2=vtm2: e.matmul(
                        env.ps[ubk][0:64, 0:64],
                        kst[:, h * 64:(h + 1) * 64],
                        vtm2[:, h * 64:(h + 1) * 64],
                        start=True, stop=True),
                        reads=[("kst",), ("vtm",)], writes=[("ps", ubk)])
                    p.op("V", lambda e, h=h, base=base, hs=hs, ubk=ubk:
                         e.scalar_tensor_tensor(
                             state[base:base + 64, hs * 64:(hs + 1) * 64],
                             state[base:base + 64, hs * 64:(hs + 1) * 64],
                             eLt[base:base + 64, hs:hs + 1],
                             env.ps[ubk][0:64, 0:64],
                             ALU.mult, ALU.add),
                         reads=[("st8", h), ("eLt",), ("ps", ubk)],
                         writes=[("st8", h)])
            for dc_o in range(nch):
                bk = dc_o % 2
                sp = dc_o % 2
                p.op("T", lambda e, dc_o=dc_o, bk=bk, op_=op_, tt=tt: [
                    e.matmul(env.ps[bk][:, :],
                             wo_sb2[:, dc * D + dc_o * 128:
                                    dc * D + (dc_o + 1) * 128],
                             op_[:, dc * 512:(dc + 1) * 512],
                             start=(dc == 0), stop=(dc == nch - 1))
                    for dc in range(nch)][-1],
                    reads=[("wo",)] + [("ofm", 0, hs) for hs in range(8)],
                    writes=[("ps", bk)])
                p.dma("DA", fx[sp][:, :].rearrange("p (c t) -> p c t", c=1),
                      x_h.rearrange("(c p) t -> p c t", p=128)
                      [:, dc_o:dc_o + 1, sl],
                      reads=[], writes=[("fx", sp)])
                p.op("V", lambda e, bk=bk, sp=sp: e.tensor_add(
                    fx[sp][:, :], env.ps[bk][:, :], fx[sp][:, :]),
                    reads=[("ps", bk), ("fx", sp)], writes=[("fx", sp)])
                p.dma("DY", y_h.rearrange("(c p) t -> p c t", p=128)
                      [:, dc_o:dc_o + 1, sl],
                      fx[sp][:, :].rearrange("p (c t) -> p c t", c=1),
                      reads=[("fx", sp)], writes=[("yH", dc_o, tt)])
        if fused is None:
            p.wait_all("Y")
            block = ctx.enter_context(nc.Block())
            p.emit(block)
    return nc
    return nc



def build_fused_program(T=4096):
    nc = bass.Bass("TRN2", target_bir_lowering=False, debug=False)
    x0 = dram(nc, "x", [D, T], F32, "ExternalInput")
    y_h = dram(nc, "y", [D, T], F32, "ExternalOutput")
    xa = dram(nc, "chainA", [D, T], F32, "Internal")
    xb = dram(nc, "chainB", [D, T], F32, "Internal")
    with ExitStack() as ctx:
        p = Prog(nc, ctx)
        env = Env(nc, ctx, p)
        pool = Pool(env, POOL_COLS)
        fz = {"nc": nc, "ctx": ctx, "p": p, "env": env, "pool": pool}
        chain = [
            (build_a0_program, "a0_"), (build_f0_program, "f0_"),
            (build_b0_program, "b0_"), (build_f0_program, "f0_"),
            (build_a1_program, "a1_"), (build_f1_program, "f1_"),
            (build_b1_program, "b1_"), (build_f1_program, "f1_"),
        ]
        cur = x0
        pp = [xa, xb]
        for i, (fn, pfx) in enumerate(chain):
            out = y_h if i == len(chain) - 1 else pp[i % 2]
            fn(T=T, fused=dict(fz, x=cur, y=out), pfx=pfx)
            cur = out
        p.ns = None
        p.wait_all("Y")
        block = ctx.enter_context(nc.Block())
        p.emit(block)
    return nc



_PROG_CACHE = {}


def _prog(name, builder):
    if name not in _PROG_CACHE:
        _PROG_CACHE[name] = builder()
    return _PROG_CACHE[name]


def _c2(v):
    return np.ascontiguousarray(np.asarray(v, np.float32).reshape(-1, 128).T)


def _c2s(v):
    return np.stack([_c2(v[i]) for i in range(v.shape[0])])


def _pack_mix(m):
    A = np.zeros((128, 512), np.float32)
    for h in range(16):
        A[(h % 2) * 64:(h % 2) * 64 + 64,
          (h // 2) * 64:(h // 2) * 64 + 64] = m[h].T
    return A


def _rope_tables(T, dim=64):
    inv = 1.0 / 10000.0 ** (np.arange(0, dim, 2, dtype=np.float32) / dim)
    t = np.arange(T, dtype=np.float32)
    fr = t[:, None] * inv[None, :]
    emb = np.concatenate([fr, fr], -1)
    return np.cos(emb).astype(np.float32), np.sin(emb).astype(np.float32)


def _rot_matrix():
    M = np.zeros((128, 128), np.float32)
    for blk in range(2):
        o = blk * 64
        for j in range(32):
            M[o + j + 32, o + j] = -1.0
            M[o + j, o + j + 32] = 1.0
    return M


def kernel(**ii):
    f32 = np.float32
    cores = list(range(N_CORES))
    xs = [np.ascontiguousarray(np.asarray(ii["L0_attn_a_x"][b], f32).T)
          for b in range(N_CORES)]

    cos, sin = _rope_tables(T)
    a0_w = {
        "wqkv": np.asarray(ii["L0_attn_a_Wqkv"], f32),
        "bqkv2": _c2(ii["L0_attn_a_bqkv"]),
        "bqkvr": np.asarray(ii["L0_attn_a_bqkv"], f32)[None, :],
        "wo": np.asarray(ii["L0_attn_a_Wo"], f32),
        "bo2": _c2(ii["L0_attn_a_bo"]),
        "w1": np.asarray(ii["L0_attn_a_W1"], f32),
        "b1_2": _c2(ii["L0_attn_a_b1"]),
        "w2": np.asarray(ii["L0_attn_a_W2"], f32),
        "b2_2": _c2(ii["L0_attn_a_b2"]),
        "ln1g2": _c2(ii["L0_attn_a_ln1_g"]), "ln1b2": _c2(ii["L0_attn_a_ln1_b"]),
        "ln2g2": _c2(ii["L0_attn_a_ln2_g"]), "ln2b2": _c2(ii["L0_attn_a_ln2_b"]),
        "cosb": np.ascontiguousarray(np.concatenate([cos.T, cos.T], 0)),
        "sinb": np.ascontiguousarray(np.concatenate([sin.T, sin.T], 0)),
        "rotm": _rot_matrix(),
    }
    f0_w = {
        "rw": np.asarray(ii["L0_ffn_router_w"], f32),
        "rbias": np.asarray(ii["L0_ffn_router_b"], f32)[None, :],
        "up": np.ascontiguousarray(
            np.asarray(ii["L0_ffn_expert_up"], f32).transpose(0, 2, 1)),
        "dn": np.ascontiguousarray(
            np.asarray(ii["L0_ffn_expert_down"], f32).transpose(0, 2, 1)),
    }
    b0_w = {
        "wdn": np.asarray(ii["L0_attn_b_W_down"], f32),
        "tokp": _pack_mix(np.asarray(ii["L0_attn_b_tokeys"], f32)),
        "toqp": _pack_mix(np.asarray(ii["L0_attn_b_toqueries"], f32)),
        "tovp": _pack_mix(np.asarray(ii["L0_attn_b_tovalues"], f32)),
        "wu": np.asarray(ii["L0_attn_b_W_unify"], f32),
        "bu2": _c2(ii["L0_attn_b_b_unify"]),
    }
    a1_w = {
        "wq": np.asarray(ii["L1_attn_a_wq"], f32),
        "wk": np.asarray(ii["L1_attn_a_wk"], f32),
        "wv": np.asarray(ii["L1_attn_a_wv"], f32),
        "wo": np.asarray(ii["L1_attn_a_wo"], f32),
        "bq2": _c2s(np.asarray(ii["L1_attn_a_bq"], f32)),
        "bk2": _c2s(np.asarray(ii["L1_attn_a_bk"], f32)),
        "bvr": np.asarray(ii["L1_attn_a_bv"], f32)[:, None, :],
        "bo2": _c2s(np.asarray(ii["L1_attn_a_bo"], f32)),
        "w1": np.asarray(ii["L1_attn_a_w1"], f32),
        "b1_2": _c2s(np.asarray(ii["L1_attn_a_b1"], f32)),
        "w2": np.asarray(ii["L1_attn_a_w2"], f32),
        "b2_2": _c2s(np.asarray(ii["L1_attn_a_b2"], f32)),
        "g1": _c2s(np.asarray(ii["L1_attn_a_ln1_g"], f32)),
        "be1": _c2s(np.asarray(ii["L1_attn_a_ln1_b"], f32)),
        "g2": _c2s(np.asarray(ii["L1_attn_a_ln2_g"], f32)),
        "be2": _c2s(np.asarray(ii["L1_attn_a_ln2_b"], f32)),
        "gf2": _c2(ii["L1_attn_a_lnf_g"]), "bef2": _c2(ii["L1_attn_a_lnf_b"]),
        "m3r": (np.arange(516) % 3 != 0).astype(f32)[None, :],
    }
    f1_w = {
        "wg2": np.ascontiguousarray(
            np.asarray(ii["L1_ffn_Wg"], f32)[:, 0].reshape(8, 128).T),
        "wq": np.asarray(ii["L1_ffn_Wq"], f32),
        "wk": np.asarray(ii["L1_ffn_Wk"], f32),
        "wv": np.asarray(ii["L1_ffn_Wv"], f32),
        "wo": np.asarray(ii["L1_ffn_Wo"], f32),
        "iotac": (np.arange(128, dtype=f32) + 1)[:, None],
    }
    b1_w = {
        "qw": np.ascontiguousarray(np.asarray(ii["L1_attn_b_q_w"], f32).T),
        "kw": np.ascontiguousarray(np.asarray(ii["L1_attn_b_k_w"], f32).T),
        "vw": np.ascontiguousarray(np.asarray(ii["L1_attn_b_v_w"], f32).T),
        "ow": np.ascontiguousarray(np.asarray(ii["L1_attn_b_o_w"], f32).T),
        "cs1": np.ascontiguousarray(np.asarray(ii["L1_attn_b_cs_w1"], f32).T),
        "csb1_2": _c2(ii["L1_attn_b_cs_b1"]),
        "cs2": np.ascontiguousarray(np.asarray(ii["L1_attn_b_cs_w2"], f32).T),
        "csb2c": np.asarray(ii["L1_attn_b_cs_b2"], f32)[:, None],
        "dpr": np.asarray(ii["L1_attn_b_decay_params"], f32).reshape(1, D),
    }

    weights = {}
    for pfx, w in (("a0_", a0_w), ("f0_", f0_w), ("b0_", b0_w),
                   ("a1_", a1_w), ("f1_", f1_w), ("b1_", b1_w)):
        for k, v in w.items():
            weights[pfx + k] = v
    nc_prog = _prog("fused", build_fused_program)
    in_maps = [dict(weights, x=xs[b]) for b in range(N_CORES)]
    res = run_bass_kernel_spmd(nc_prog, in_maps, core_ids=cores)
    return np.stack([np.asarray(res.results[b]["y"], f32).T
                     for b in range(N_CORES)], 0)
